# version 06
# baseline (speedup 1.0000x reference)
import numpy as np
import ml_dtypes
import concourse.bass as bass
import concourse.mybir as mybir
from concourse.bass_utils import run_bass_kernel_spmd

F32 = mybir.dt.float32
BF16 = mybir.dt.bfloat16
AF = mybir.ActivationFunctionType
ALU = mybir.AluOpType
AX = mybir.AxisListType

D = 2048
DC = 16
B = 4
S = 4096
NCORES = 8
TCORE = 2048
NHEADS = 16
HD = 128
H0 = 8192; H0C = 64
H1 = 5632; H1C = 44
QKVC = 48
BLK = 1024
TT = 512
NTT = TCORE // TT
LNEPS = 1e-5
ZIG = {0: (0, 3), 1: (1, 2)}
GATHER_BLOCKS = (0, 3, 1, 2)
GATHER_BASE = {0: 0, 3: 1024, 1: 2048, 2: 3072}

ENGS = ("sync", "gpsimd", "tensor", "scalar", "vector")


class Prog:

    def __init__(self, nc):
        self.nc = nc
        self.ops = {e: [] for e in ENGS}
        self.sems = {}
        self.cnt = {}
        self.segments = [("block", self.ops)]
        self._switch = None

    def sem(self, name):
        if name not in self.sems:
            self.sems[name] = self.nc.alloc_semaphore(name)
        if name not in self.cnt:
            self.cnt[name] = 0
        return self.sems[name]

    def wait(self, eng, name, val):
        h = self.sem(name)
        self.ops[eng].append(lambda e, h=h, v=val: e.wait_ge(h, v))

    ENGINE_SEM = {"scalar": "ACT", "vector": "DVE", "gpsimd": "POOL"}

    def op(self, eng, fn, inc=None, n=1, serial=True, dma=False):
        own = None if dma else self.ENGINE_SEM.get(eng)
        if own is not None:
            assert inc is None or inc == own, f"{eng} ops must inc {own}"
            if serial:
                h0 = self.sem(own)
                v0 = self.cnt[own]
                self.ops[eng].append(lambda e, h=h0, v=v0: e.wait_ge(h, v))
            inc = own
            n = 1
        v = None
        h = None
        if inc is not None:
            h = self.sem(inc)
            self.cnt[inc] += n
            v = self.cnt[inc]
        def run(e, fn=fn, h=h, n=n):
            inst = fn(e)
            if h is not None:
                inst.then_inc(h, n)
        self.ops[eng].append(run)
        return v

    def dma(self, eng, out_ap, in_ap, inc):
        return self.op(eng, lambda e, o=out_ap, i=in_ap: e.dma_start(out=o, in_=i),
                       inc=inc, n=16, dma=True)

    def begin_switch(self):
        self._switch = {"snap_cnt": dict(self.cnt), "bodies": [], "ends": []}
        self.segments.append(("switch", self._switch["bodies"]))

    def begin_body(self):
        self.cnt = dict(self._switch["snap_cnt"])
        self.ops = {e: [] for e in ENGS}
        self._switch["bodies"].append(self.ops)

    def end_body(self):
        self._switch["ends"].append(dict(self.cnt))

    def end_switch(self):
        ends = self._switch["ends"]
        for other in ends[1:]:
            assert other == ends[0], f"switch bodies disagree on sem counts:\n{ends[0]}\n{other}"
        self.cnt = ends[0]
        self._switch = None
        self.ops = {e: [] for e in ENGS}
        self.segments.append(("block", self.ops))

    def finish(self):
        nc = self.nc
        handles = {"sync": nc.sync, "gpsimd": nc.gpsimd, "tensor": nc.tensor,
                   "scalar": nc.scalar, "vector": nc.vector}
        for kind, payload in self.segments:
            if kind == "block":
                if not any(payload[e] for e in ENGS):
                    continue
                with nc.Block() as block:
                    for ename in ENGS:
                        getattr(block, ename)(
                            lambda e, fs=tuple(payload[ename]): [f(e) for f in fs])
            else:
                engines = [handles[e] for e in ENGS]
                regs = []
                for eng in engines:
                    pid = eng.partition_id()
                    regs.append(eng.compute_val(pid % 2))
                for i in nc.Switch(engines=engines, index=regs, n=len(payload)):
                    body = payload[i]
                    for ename in ENGS:
                        e = handles[ename]
                        for f in body[ename]:
                            f(e)
        return nc


class Pool:

    def __init__(self, nc):
        self.nc = nc
        self.xt = nc.alloc_sbuf_tensor("xt", [128, DC * TT], F32)
        self.xn = [nc.alloc_sbuf_tensor(f"xn{i}", [128, DC * TT], BF16) for i in range(2)]
        self.xbf = [nc.alloc_sbuf_tensor(f"xbf{i}", [128, TT], BF16) for i in range(2)]
        self.xsq = [nc.alloc_sbuf_tensor(f"xsq{i}", [128, TT], BF16) for i in range(2)]
        self.rows = nc.alloc_sbuf_tensor("rows", [1, 6 * TT], F32)
        self.tmp32 = [nc.alloc_sbuf_tensor(f"tmp32_{i}", [128, TT], F32) for i in range(2)]
        self.hid = nc.alloc_sbuf_tensor("hid", [128, H0C * TT], BF16)
        self.wup = [[nc.alloc_sbuf_tensor(f"wup{w}_{p}", [128, DC * 256], BF16)
                     for p in range(2)] for w in range(2)]
        self.wdn = [nc.alloc_sbuf_tensor(f"wdn{p}", [128, 1024], BF16) for p in range(2)]
        self.stage = [nc.alloc_sbuf_tensor(f"stage{i}", [128, TT], BF16) for i in range(4)]
        self.ones_col = nc.alloc_sbuf_tensor("ones_col", [128, 1], BF16)
        self.ones_row = nc.alloc_sbuf_tensor("ones_row", [1, 128], F32)
        self.scale_cols = nc.alloc_sbuf_tensor("scale_cols", [128, 3 * 64], F32)
        self.qbuf = nc.alloc_sbuf_tensor("qbuf", [128, 2 * 2048], BF16)
        self.msk = nc.alloc_sbuf_tensor("msk", [128, 4 * TT], BF16)
        self.eye = nc.alloc_sbuf_tensor("eye", [128, 128], BF16)
        self.qk32 = self.tmp32
        self.qk2 = nc.alloc_sbuf_tensor("qk2", [128, 2 * TT], F32)
        self.swp = nc.alloc_sbuf_tensor("swp", [128, 128], F32)
        self.colbuf = nc.alloc_sbuf_tensor("colbuf", [128, 4], F32)
        self.lncol = nc.alloc_sbuf_tensor("lncol", [128, 2], F32)
        self.ps = [nc.alloc_psum_tensor(f"ps{i}", [128, 512], F32) for i in range(8)]


def stage_ln_tile(P, sb, X_in, tt, par, gate_prev=None):
    tsl = slice(tt * TT, (tt + 1) * TT)
    if gate_prev is not None:
        P.wait("sync", gate_prev[0], gate_prev[1])
    xin_ap = X_in.ap()[:, :, tsl].rearrange("c p t -> p c t")
    v_xload = P.dma("sync", sb.xt[:], xin_ap, "XL")

    P.wait("scalar", "XL", v_xload)
    pe_after_cast = {}
    for c in range(DC):
        pp = c % 2
        if c >= 2:
            P.wait("scalar", "PE", pe_after_cast[c - 2])
        xc = sb.xt[:, c * TT:(c + 1) * TT]
        P.op("scalar", lambda e, o=sb.xbf[pp][:], i=xc: e.activation(o, i, AF.Copy))
        v_act = P.op("scalar", lambda e, o=sb.xsq[pp][:], i=xc: e.activation(o, i, AF.Square),
                     inc="ACT")
        P.wait("tensor", "ACT", v_act)
        P.op("tensor", lambda e, o=sb.ps[6][0:1, :], l=sb.ones_col[:], r=sb.xbf[pp][:],
             s=(c == 0), t=(c == DC - 1): e.matmul(o, l, r, start=s, stop=t))
        pe_after_cast[c] = P.op(
            "tensor", lambda e, o=sb.ps[7][0:1, :], l=sb.ones_col[:], r=sb.xsq[pp][:],
            s=(c == 0), t=(c == DC - 1): e.matmul(o, l, r, start=s, stop=t),
            inc="PE")
    v_stats = pe_after_cast[DC - 1]

    r = lambda i: sb.rows[0:1, i * TT:(i + 1) * TT]
    P.wait("vector", "PE", v_stats)
    P.op("vector", lambda e: e.tensor_scalar_mul(r(0), sb.ps[6][0:1, :], 1.0 / D))
    P.op("vector", lambda e: e.tensor_scalar_mul(r(1), sb.ps[7][0:1, :], 1.0 / D))
    P.op("vector", lambda e: e.tensor_mul(r(2), r(0), r(0)))
    P.op("vector", lambda e: e.tensor_sub(r(2), r(1), r(2)))
    P.op("vector", lambda e: e.tensor_scalar_add(r(2), r(2), LNEPS))
    v_dve = P.op("vector", lambda e: e.reciprocal(r(3), r(2)), inc="DVE")
    P.wait("scalar", "DVE", v_dve)
    P.wait("scalar", "PE", P.cnt["PE"])
    v_sqrt = P.op("scalar", lambda e: e.activation(r(4), r(3), AF.Sqrt), inc="ACT")
    P.wait("vector", "ACT", v_sqrt)
    v_mr = P.op("vector", lambda e: e.tensor_mul(r(5), r(0), r(4)), inc="DVE")

    P.wait("tensor", "DVE", v_mr)
    P.op("tensor", lambda e: e.matmul(sb.ps[6][:, 0:TT], sb.ones_row[0:1, :], r(4),
                                      start=True, stop=True))
    v_bc = P.op("tensor", lambda e: e.matmul(sb.ps[7][:, 0:TT], sb.ones_row[0:1, :], r(5),
                                             start=True, stop=True), inc="PE")

    P.wait("vector", "PE", v_bc)
    v_apply = None
    for c in range(DC):
        xc = sb.xt[:, c * TT:(c + 1) * TT]
        xnc = sb.xn[par][:, c * TT:(c + 1) * TT]
        tp = sb.tmp32[c % 2][:]
        P.op("vector", lambda e, tp=tp, xc=xc: e.tensor_mul(tp, xc, sb.ps[6][:, 0:TT]))
        v_apply = P.op("vector", lambda e, o=xnc, tp=tp: e.tensor_sub(o, tp, sb.ps[7][:, 0:TT]),
                       inc="DVE")
    return {"xload": v_xload, "xn_done": v_apply}


def stage_ffn_tile(P, sb, X_out, tt, par, xn_done, wcfg, state):
    HC = wcfg["HC"]
    NHG = HC // 2
    tsl = slice(tt * TT, (tt + 1) * TT)
    P.wait("tensor", "DVE", xn_done)

    for hg in range(NHG):
        wpar = hg % 2
        for w in range(2):
            Wd = wcfg["Wg"] if w == 0 else wcfg["Wu"]
            buf = sb.wup[w][wpar]
            last = state["wbuf_last_pe"].get((w, wpar))
            if last is not None:
                P.wait("sync", "PE", last)
            src = Wd.ap().rearrange("(k p) h -> p k h", p=128)[:, :, hg * 256:(hg + 1) * 256]
            v_w = None
            for kg in range(4):
                v_w = P.dma("sync", buf[:].rearrange("p (k h) -> p k h", h=256)[:, kg * 4:(kg + 1) * 4, :],
                            src[:, kg * 4:(kg + 1) * 4, :], f"WU{wpar}")
            state["wbuf_ready"][(w, wpar)] = v_w
        for hc in range(2):
            c = hg * 2 + hc
            bg = sb.ps[(c % 2) * 2 + 0]
            bu = sb.ps[(c % 2) * 2 + 1]
            lastg = state["psgu_last"].get(c % 2)
            if lastg is not None:
                P.wait("tensor", "DVE", lastg)
            if hc == 0:
                P.wait("tensor", f"WU{wpar}", state["wbuf_ready"][(1, wpar)])
            pe_done = {}
            for w in range(2):
                bank = bg if w == 0 else bu
                buf = sb.wup[w][wpar]
                for k in range(DC):
                    lhsT = buf[:, k * 256 + hc * 128: k * 256 + hc * 128 + 128]
                    rhs = sb.xn[par][:, k * TT:(k + 1) * TT]
                    vv = P.op("tensor",
                              lambda e, o=bank[:], l=lhsT, r=rhs, s=(k == 0), t=(k == DC - 1):
                              e.matmul(o, l, r, start=s, stop=t),
                              inc="PE" if k == DC - 1 else None)
                pe_done[w] = vv
            if hc == 1:
                state["wbuf_last_pe"][(0, wpar)] = pe_done[0]
                state["wbuf_last_pe"][(1, wpar)] = pe_done[1]
            tg = sb.tmp32[c % 2][:]
            P.wait("scalar", "PE", pe_done[1])
            gs = wcfg["g_scale"](c) if wcfg["g_scale"] else None
            us = wcfg["u_scale"](c) if wcfg["u_scale"] else None
            hc_ap = sb.hid[:, c * TT:(c + 1) * TT]
            if gs is not None:
                v_silu = P.op("scalar", lambda e, o=tg, i=bg[:], s=gs:
                              e.activation(o, i, AF.Silu, scale=s), inc="ACT")
            else:
                v_silu = P.op("scalar", lambda e, o=tg, i=bg[:]:
                              e.activation(o, i, AF.Silu), inc="ACT")
            if us is not None:
                tu = sb.stage[c % 2][:]
                v_act = P.op("scalar", lambda e, o=tu, i=bu[:], s=us:
                             e.activation(o, i, AF.Copy, scale=s), inc="ACT")
                P.wait("vector", "ACT", v_act)
                v_h = P.op("vector", lambda e, o=hc_ap, a=tg, b=tu: e.tensor_mul(o, a, b), inc="DVE")
            else:
                P.wait("vector", "ACT", v_silu)
                v_h = P.op("vector", lambda e, o=hc_ap, a=tg, b=bu[:]: e.tensor_mul(o, a, b), inc="DVE")
            state["psgu_last"][c % 2] = v_h
            state["hid_done"] = v_h

    v_hid = state["hid_done"]
    P.wait("tensor", "DVE", v_hid)
    for dmh in range(2):
        if dmh == 1:
            P.wait("tensor", "DVE", P.cnt["DVE"])
        bank_pe_done = {}
        for f in range(HC):
            wpar = f % 2
            last = state["wdn_last_pe"].get(wpar)
            if last is not None:
                P.wait("sync", "PE", last)
            v_w = P.dma("sync", sb.wdn[wpar][:],
                        wcfg["Wdn"].ap()[f * 128:(f + 1) * 128, dmh * 1024:(dmh + 1) * 1024],
                        f"WD{wpar}")
            P.wait("tensor", f"WD{wpar}", v_w)
            for m8 in range(8):
                lhsT = sb.wdn[wpar][:, m8 * 128:(m8 + 1) * 128]
                rhs = sb.hid[:, f * TT:(f + 1) * TT]
                vv = P.op("tensor",
                          lambda e, o=sb.ps[m8][:], l=lhsT, r=rhs, s=(f == 0), t=(f == HC - 1):
                          e.matmul(o, l, r, start=s, stop=t),
                          inc="PE" if (f == HC - 1 or m8 == 7) else None)
                if f == HC - 1:
                    bank_pe_done[m8] = vv
            state["wdn_last_pe"][wpar] = P.cnt["PE"]
        for m8 in range(8):
            c = dmh * 8 + m8
            xc = sb.xt[:, c * TT:(c + 1) * TT]
            P.wait("vector", "PE", bank_pe_done[m8])
            ds = wcfg["d_scale"](c) if wcfg["d_scale"] else 1.0
            P.op("vector",
                 lambda e, o=xc, i=sb.ps[m8][:], s=ds:
                 e.scalar_tensor_tensor(o, i, s, o, ALU.mult, ALU.add),
                 inc="DVE")
    v_evac = P.cnt["DVE"]
    P.wait("gpsimd", "DVE", v_evac)
    xout_ap = X_out.ap()[:, :, tsl].rearrange("c p t -> p c t")
    v_store = P.dma("gpsimd", xout_ap, sb.xt[:], "XS")
    return v_store


def stage_cast_weight(P, sb, src_f32, dst_bf16, rows, cols, state, staging=None):
    assert rows % 128 == 0
    if staging is None:
        staging = [(sb.hid[:, 0:H0C * TT // 2], H0C * TT // 2),
                   (sb.hid[:, H0C * TT // 2:H0C * TT], H0C * TT // 2)]
    it = 0
    for rc in range(rows // 128):
        c0 = 0
        while c0 < cols:
            par = it % 2
            buf, width = staging[par]
            cw = min(width, cols - c0)
            last = state.get(("cast_last", par))
            if last is not None:
                P.wait("gpsimd", f"CS{par}", last)
            v_in = P.dma("gpsimd", buf[:, 0:cw],
                         src_f32.ap()[rc * 128:(rc + 1) * 128, c0:c0 + cw], "CL")
            P.wait("gpsimd", "CL", v_in)
            v_out = P.dma("gpsimd", dst_bf16.ap()[rc * 128:(rc + 1) * 128, c0:c0 + cw],
                          buf[:, 0:cw], f"CS{par}")
            state[("cast_last", par)] = v_out
            c0 += cw
            it += 1
    return state[("cast_last", (it - 1) % 2)]


def stage_w2_transform(P, sb, W2T_f32, W2A_bf16, W2pp_bf16, h128_sb, h64_sb, state):
    SCALE = 0.5 / float(np.sqrt(8192.0))

    def one_pass(n_iter, lhs_sb, src_ap_of, dst_ap_of, scale):
        for it in range(n_iter):
            par = it % 2
            src = sb.wup[0][par]
            dst = sb.wup[1][par]
            last_pe = state.get(("w2src_pe", par))
            if last_pe is not None:
                P.wait("gpsimd", "PE", last_pe)
            v_in = None
            for sb_slice, dram_ap in src_ap_of(it, src):
                v_in = P.dma("gpsimd", sb_slice, dram_ap, "TL")
            P.wait("tensor", "TL", v_in)
            acts = []
            for nn in range(4):
                bidx = 4 + 2 * par + (nn % 2)
                last_ev = state.get(("w2bank", bidx))
                if last_ev is not None:
                    P.wait("tensor", "ACT", last_ev)
                vv = P.op("tensor",
                          lambda e, o=sb.ps[bidx][:], l=lhs_sb, r=src[:, nn * 512:(nn + 1) * 512]:
                          e.matmul(o, l, r, start=True, stop=True),
                          inc="PE")
                if nn == 3:
                    state[("w2src_pe", par)] = vv
                P.wait("scalar", "PE", vv)
                if nn == 0:
                    last_sg = state.get(("w2dst_sg", par))
                    if last_sg is not None:
                        P.wait("scalar", f"TS{par}", last_sg)
                v_e = P.op("scalar",
                           lambda e, o=dst[:, nn * 512:(nn + 1) * 512], i=sb.ps[bidx][:], s=scale:
                           e.activation(o, i, AF.Copy, scale=s),
                           inc="ACT")
                state[("w2bank", bidx)] = v_e
                acts.append(v_e)
            P.wait("gpsimd", "ACT", acts[-1])
            v_out = None
            for dram_ap, sb_slice in dst_ap_of(it, dst):
                v_out = P.dma("gpsimd", dram_ap, sb_slice, f"TS{par}")
            state[("w2dst_sg", par)] = v_out
        return (P.cnt.get("TS0", 0), P.cnt.get("TS1", 0))

    v_allA = one_pass(
        H0C, h128_sb,
        lambda c, src: [(src[:, 0:2048], W2T_f32.ap()[c * 128:(c + 1) * 128, :])],
        lambda c, dst: [(W2A_bf16.ap()[c * 128:(c + 1) * 128, :], dst[:, 0:2048])],
        SCALE)
    w2a_r = W2A_bf16.ap().rearrange("(a btwo) d -> a btwo d", a=64)
    w2pp_r = W2pp_bf16.ap().rearrange("(a btwo) d -> a btwo d", a=64)
    P.wait("gpsimd", "TS0", v_allA[0])
    P.wait("gpsimd", "TS1", v_allA[1])
    one_pass(
        64, h64_sb,
        lambda bhi, src: [(src[b0 * 64:(b0 + 1) * 64, 0:2048], w2a_r[:, 2 * bhi + b0, :])
                          for b0 in range(2)],
        lambda bhi, dst: [(w2pp_r[:, 2 * bhi + b0, :], dst[b0 * 64:(b0 + 1) * 64, 0:2048])
                          for b0 in range(2)],
        1.0)
    return None


def stage_prologue(P, sb, consts):
    P.op("vector", lambda e: e.memset(sb.ones_col[:], 1.0))
    P.op("vector", lambda e: e.memset(sb.ones_row[:], 1.0), inc="DVE")
    v = None
    for ap, dram in consts:
        v = P.dma("sync", ap, dram.ap(), "LW")
    return {"consts": v, "ones": P.cnt["DVE"]}


def build_ffn0_launch(T_core=TCORE):
    nc = bass.Bass("TRN2", target_bir_lowering=False)
    ntt = T_core // TT
    X_in = nc.dram_tensor("x_in", [DC, 128, T_core], F32, kind="ExternalInput")
    X_out = nc.dram_tensor("x_out", [DC, 128, T_core], F32, kind="ExternalOutput")
    W1T = nc.dram_tensor("w1t", [D, H0], F32, kind="ExternalInput")
    W3T = nc.dram_tensor("w3t", [D, H0], F32, kind="ExternalInput")
    W2T = nc.dram_tensor("w2t", [H0, D], F32, kind="ExternalInput")
    H128c = nc.dram_tensor("h128c", [128, 128], BF16, kind="ExternalInput")
    H64c = nc.dram_tensor("h64c", [128, 128], BF16, kind="ExternalInput")
    W1T16 = nc.dram_tensor("w1t16", [D, H0], BF16)
    W3T16 = nc.dram_tensor("w3t16", [D, H0], BF16)
    W2A = nc.dram_tensor("w2a", [H0, D], BF16)
    W2pp = nc.dram_tensor("w2pp", [H0, D], BF16)

    sb = Pool(nc)
    h128_sb = sb.msk.ap_handle if False else sb.msk
    h128_sb_ap = sb.msk[:, 0:128]
    h64_sb_ap = sb.msk[:, 128:256]

    P = Prog(nc)
    pro = stage_prologue(P, sb, [(h128_sb_ap, H128c), (h64_sb_ap, H64c)])
    state = {"wbuf_last_pe": {}, "wbuf_ready": {}, "psgu_last": {}, "wdn_last_pe": {}}
    P.wait("tensor", "LW", pro["consts"])
    P.wait("tensor", "DVE", pro["ones"])
    stage_w2_transform(P, sb, W2T, W2A, W2pp, h128_sb_ap, h64_sb_ap, state)
    stage_cast_weight(P, sb, W1T, W1T16, D, H0, state)
    stage_cast_weight(P, sb, W3T, W3T16, D, H0, state)
    for s in ("TS0", "TS1", "CS0", "CS1"):
        P.wait("sync", s, P.cnt.get(s, 0))

    wcfg = dict(Wg=W1T16, Wu=W3T16, Wdn=W2pp, HC=H0C,
                g_scale=None, u_scale=None, d_scale=None)
    gate = None
    for tt in range(ntt):
        par = tt % 2
        ln = stage_ln_tile(P, sb, X_in, tt, par, gate_prev=gate)
        v_store = stage_ffn_tile(P, sb, X_out, tt, par, ln["xn_done"], wcfg, state)
        gate = ("XS", v_store)
    P.wait("sync", "XS", P.cnt["XS"])
    P.finish()
    return nc


def build_ffn1_launch(T_core=TCORE):
    nc = bass.Bass("TRN2", target_bir_lowering=False)
    ntt = T_core // TT
    X_in = nc.dram_tensor("x_in", [DC, 128, T_core], F32, kind="ExternalInput")
    X_out = nc.dram_tensor("x_out", [DC, 128, T_core], F32, kind="ExternalOutput")
    GT = nc.dram_tensor("gt", [D, H1], F32, kind="ExternalInput")
    UT = nc.dram_tensor("ut", [D, H1], F32, kind="ExternalInput")
    DT = nc.dram_tensor("dt", [H1, D], F32, kind="ExternalInput")
    SCL = nc.dram_tensor("scl", [128, 3 * 64], F32, kind="ExternalInput")
    GT16 = nc.dram_tensor("gt16", [D, H1], BF16)
    UT16 = nc.dram_tensor("ut16", [D, H1], BF16)
    DT16 = nc.dram_tensor("dt16", [H1, D], BF16)

    sb = Pool(nc)
    P = Prog(nc)
    pro = stage_prologue(P, sb, [(sb.scale_cols[:], SCL)])
    state = {"wbuf_last_pe": {}, "wbuf_ready": {}, "psgu_last": {}, "wdn_last_pe": {}}
    stage_cast_weight(P, sb, GT, GT16, D, H1, state)
    stage_cast_weight(P, sb, UT, UT16, D, H1, state)
    stage_cast_weight(P, sb, DT, DT16, H1, D, state)
    for s in ("CS0", "CS1"):
        P.wait("sync", s, P.cnt.get(s, 0))
    P.wait("scalar", "LW", pro["consts"])
    P.wait("tensor", "DVE", pro["ones"])

    wcfg = dict(Wg=GT16, Wu=UT16, Wdn=DT16, HC=H1C,
                g_scale=lambda c: sb.scale_cols[:, c:c + 1],
                u_scale=lambda c: sb.scale_cols[:, 64 + c:64 + c + 1],
                d_scale=lambda c: sb.scale_cols[:, 128 + c:128 + c + 1])
    gate = None
    for tt in range(ntt):
        par = tt % 2
        ln = stage_ln_tile(P, sb, X_in, tt, par, gate_prev=gate)
        v_store = stage_ffn_tile(P, sb, X_out, tt, par, ln["xn_done"], wcfg, state)
        gate = ("XS", v_store)
    P.wait("sync", "XS", P.cnt["XS"])
    P.finish()
    return nc


def attn_geometry(S_):
    BLK_ = S_ // 4
    TC = S_ // 2
    nkb = S_ // 128
    per = BLK_ // 128
    order = [0, 3, 1, 2]
    gather_abs_base = {}
    for slot, ab in enumerate(order):
        for j in range(per):
            gather_abs_base[slot * per + j] = ab * BLK_ + j * 128
    qpb = BLK_ // TT
    qc_info = {}
    for parity in (0, 1):
        blocks = ZIG[parity]
        for qc in range(TC // TT):
            abs_qbase = blocks[qc // qpb] * BLK_ + (qc % qpb) * TT
            kbs = []
            for g in range(nkb):
                kb = gather_abs_base[g]
                if kb > abs_qbase + TT - 1:
                    continue
                off = kb - abs_qbase
                mask_idx = off // 128 if off >= 0 else None
                kbs.append((g, mask_idx))
            qc_info[(parity, qc)] = (abs_qbase, kbs)
    return TC, BLK_, gather_abs_base, qc_info


def stage_qkv_tile(P, sb, att, tt, par, xn_done, acfg, state, TC):
    tsl = slice(tt * TT, (tt + 1) * TT)
    P.wait("tensor", "DVE", xn_done)
    if acfg["variant"] == 1:
        rv = sb.hid[:, 0:4 * TT].bitcast(F32)
        P.wait("sync", "DVE", state.get("rope_used", 0))
        P.dma("sync", rv[:, 0:TT], att["ropec"].ap()[:, tsl], "RP")
        v_rope = P.dma("sync", rv[:, TT:2 * TT], att["ropes"].ap()[:, tsl], "RP")
        state["rope_v"] = v_rope

    for og in range(QKVC // 2):
        bpar = og % 4
        w, wp = bpar // 2, bpar % 2
        buf = sb.wup[w][wp]
        last = state["wbuf_last_pe"].get(("q", bpar))
        if last is not None:
            P.wait("sync", "PE", last)
        src = att["Wq16"].ap().rearrange("(k p) h -> p k h", p=128)[:, :, og * 256:(og + 1) * 256]
        v_w = None
        for kg in range(4):
            v_w = P.dma("sync", buf[:].rearrange("p (k h) -> p k h", h=256)[:, kg * 4:(kg + 1) * 4, :],
                        src[:, kg * 4:(kg + 1) * 4, :], f"WQ{bpar}")
        for hc in range(2):
            oc = og * 2 + hc
            kind = oc // NHEADS
            h = oc % NHEADS
            bank = sb.ps[oc % 2]
            lastb = state["psq_last"].get(oc % 2)
            if lastb is not None:
                P.wait("tensor", "ACT", lastb)
            if hc == 0:
                P.wait("tensor", f"WQ{bpar}", v_w)
            for k in range(DC):
                lhsT = buf[:, k * 256 + hc * 128: k * 256 + hc * 128 + 128]
                rhs = sb.xn[par][:, k * TT:(k + 1) * TT]
                vv = P.op("tensor",
                          lambda e, o=bank[:], l=lhsT, r=rhs, s=(k == 0), t=(k == DC - 1):
                          e.matmul(o, l, r, start=s, stop=t),
                          inc="PE" if k == DC - 1 else None)
            if hc == 1:
                state["wbuf_last_pe"][("q", bpar)] = vv
            P.wait("scalar", "PE", vv)
            st = sb.stage[oc % 4][:]
            lastst = state["stage_sg"].get(oc % 4)
            if lastst is not None:
                P.wait("scalar", f"SO{oc % 4}", lastst)
            if kind == 2:
                v_c = P.op("scalar", lambda e, o=st, i=bank[:]: e.activation(o, i, AF.Copy))
                tpar = oc % 2
                lasttr = state["tr_last"].get(tpar)
                if lasttr is not None:
                    P.wait("tensor", "DVE", lasttr)
                P.wait("tensor", "ACT", v_c)
                vt = None
                for j in range(4):
                    reg = sb.ps[2][:, tpar * 256 + j * 64: tpar * 256 + (j + 1) * 64].bitcast(BF16)
                    vt = P.op("tensor",
                              lambda e, o=reg, i=st[:, j * 128:(j + 1) * 128]:
                              e.transpose(o, i, sb.eye[:]),
                              inc="PE")
                P.wait("vector", "PE", vt)
                big = sb.ps[2][:, tpar * 256: tpar * 256 + 256].bitcast(BF16)
                v_cp = P.op("vector", lambda e, o=st, i=big: e.tensor_copy(o, i))
                state["tr_last"][tpar] = v_cp
                P.wait("gpsimd", "DVE", v_cp)
                state["psq_last"][oc % 2] = v_c
                dst = att["CCVi"].ap()[h * TC + tt * TT: h * TC + (tt + 1) * TT, :]
                dst = dst.rearrange("(j p) d -> p j d", p=128)
                v_out = P.dma("gpsimd", dst, st.rearrange("p (j d) -> p j d", d=128), f"SO{oc % 4}")
                state["stage_sg"][oc % 4] = v_out
            else:
                if acfg["variant"] == 0:
                    scale = acfg["qk_scale"]
                    v_e = P.op("scalar", lambda e, o=st, i=bank[:], s=scale:
                               e.activation(o, i, AF.Copy, scale=s))
                    state["psq_last"][oc % 2] = v_e
                    P.wait("gpsimd", "ACT", v_e)
                else:
                    v_e, v_cpy = _rope_headln(P, sb, att, bank, st, par, oc % 2, state)
                    state["psq_last"][oc % 2] = v_cpy
                    P.wait("gpsimd", "DVE", v_e)
                if kind == 0:
                    dst = att["Qd"].ap()[h, :, tsl]
                else:
                    dst = att["CCKi"].ap()[h * 128:(h + 1) * 128, tsl]
                v_out = P.dma("gpsimd", dst, st, f"SO{oc % 4}")
                state["stage_sg"][oc % 4] = v_out
    return


def _rope_headln(P, sb, att, bank, st, par, pr, state):
    q32 = sb.tmp32[pr][:]
    r32 = sb.qk2[:, pr * TT:(pr + 1) * TT]
    bs, bb2 = (sb.ps[4], sb.ps[5]) if pr == 0 else (sb.ps[6], sb.ps[7])
    rb = 3 * pr
    r = lambda i: sb.rows[0:1, (rb + i) * TT:(rb + i + 1) * TT]
    rv = sb.hid[:, 0:4 * TT].bitcast(F32)
    cc = rv[:, 0:TT]
    ss = rv[:, TT:2 * TT]
    P.wait("scalar", "RP", state["rope_v"])
    q32g = state.get(("rp_q32", pr))
    if q32g is not None:
        P.wait("scalar", "DVE", q32g[0])
        P.wait("scalar", "PE", q32g[1])
    v_c = P.op("scalar", lambda e, o=q32, i=bank[:]: e.activation(o, i, AF.Copy))
    P.wait("tensor", "ACT", v_c)
    lastr = state.get("rope_ps")
    if lastr is not None:
        P.wait("tensor", "DVE", lastr)
    v_sw = P.op("tensor", lambda e: e.matmul(sb.ps[3][:], sb.swp[:], q32,
                                             start=True, stop=True), inc="PE")
    P.wait("vector", "PE", v_sw)
    r32g = state.get(("rp_r32", pr))
    if r32g is not None:
        P.wait("vector", "ACT", r32g)
    v_r1 = P.op("vector", lambda e: e.tensor_mul(r32, q32, cc))
    v_r2 = P.op("vector", lambda e: e.tensor_mul(q32, sb.ps[3][:], ss))
    v_rope = P.op("vector", lambda e: e.tensor_add(r32, r32, q32))
    state["rope_ps"] = v_r2
    state["rope_used"] = v_rope
    P.wait("scalar", "DVE", v_rope)
    xbg = state.get(("rp_xbf", pr))
    if xbg is not None:
        P.wait("scalar", "PE", xbg)
    v_b = P.op("scalar", lambda e, o=sb.xbf[pr][:], i=r32: e.activation(o, i, AF.Copy))
    v_s = P.op("scalar", lambda e, o=sb.xsq[pr][:], i=r32: e.activation(o, i, AF.Square))
    state[("rp_r32", pr)] = v_s
    P.wait("tensor", "ACT", v_s)
    bkg = state.get(("rp_bank", pr))
    if bkg is not None:
        P.wait("tensor", "DVE", bkg)
    P.op("tensor", lambda e, o=bs[0:1, :]: e.matmul(o, sb.ones_col[:], sb.xbf[pr][:],
                                                    start=True, stop=True))
    v_mm = P.op("tensor", lambda e, o=bb2[0:1, :]: e.matmul(o, sb.ones_col[:], sb.xsq[pr][:],
                                                            start=True, stop=True), inc="PE")
    state[("rp_xbf", pr)] = v_mm
    P.wait("vector", "PE", v_mm)
    rwg = state.get(("rp_rows", pr))
    if rwg is not None:
        P.wait("vector", "PE", rwg)
    P.op("vector", lambda e, i=bs[0:1, :]: e.tensor_scalar_mul(r(0), i, 1.0 / 128.0))
    P.op("vector", lambda e, i=bb2[0:1, :]: e.tensor_scalar(r(1), i, 1.0 / 128.0, LNEPS,
                                                            ALU.mult, ALU.add))
    P.op("vector", lambda e: e.tensor_mul(r(2), r(0), r(0)))
    P.op("vector", lambda e: e.tensor_sub(r(1), r(1), r(2)))
    v_rec = P.op("vector", lambda e: e.reciprocal(r(2), r(1)))
    P.wait("scalar", "DVE", v_rec)
    v_sq = P.op("scalar", lambda e: e.activation(r(1), r(2), AF.Sqrt))
    P.wait("vector", "ACT", v_sq)
    v_mr = P.op("vector", lambda e: e.tensor_mul(r(2), r(0), r(1)))
    P.wait("tensor", "DVE", v_mr)
    P.op("tensor", lambda e, o=bs[:, 0:TT]: e.matmul(o, sb.ones_row[0:1, :], r(1),
                                                     start=True, stop=True))
    v_bc = P.op("tensor", lambda e, o=bb2[:, 0:TT]: e.matmul(o, sb.ones_row[0:1, :], r(2),
                                                             start=True, stop=True), inc="PE")
    state[("rp_rows", pr)] = v_bc
    P.wait("vector", "PE", v_bc)
    P.op("vector", lambda e, b1=bs[:, 0:TT]: e.tensor_mul(q32, r32, b1))
    v_ap = P.op("vector", lambda e, b2=bb2[:, 0:TT]: e.tensor_sub(q32, q32, b2))
    state[("rp_bank", pr)] = v_ap
    v_gb = P.op("vector", lambda e, o=st: e.tensor_scalar(o, q32, sb.lncol[:, 0:1],
                                                          sb.lncol[:, 1:2],
                                                          ALU.mult, ALU.add))
    state[("rp_q32", pr)] = (v_gb, v_sw)
    return v_gb, v_c


def stage_kv_gather(P, att, pairs):
    for i in range(4):
        P.wait("gpsimd", f"SO{i}", P.cnt.get(f"SO{i}", 0))
    P.op("gpsimd", lambda e: e.collective_compute(
        "AllGather", ALU.bypass, replica_groups=pairs,
        ins=[att["CCKi"].ap()], outs=[att["CCKo"].ap()]), inc="CC", n=1, dma=True)
    P.op("gpsimd", lambda e: e.collective_compute(
        "AllGather", ALU.bypass, replica_groups=pairs,
        ins=[att["CCVi"].ap()], outs=[att["CCVo"].ap()]), inc="CC", n=1, dma=True)
    return P.cnt["CC"]


def stage_attn_core(P, sb, att, acfg, state, TC, S_, parity_sym, v_cc):
    _, _, _, qc_info = attn_geometry(S_)
    nqc = TC // TT
    nk = S_ // 128
    r = lambda i: sb.rows[0:1, i * TT:(i + 1) * TT]
    P.wait("sync", "CC", v_cc)
    for h in range(NHEADS):
        par = h % 2
        last = state.get(("kv_pe", par))
        if last is not None:
            P.wait("sync", "PE", last)
        qa = sb.qbuf[:, par * 2048: par * 2048 + TC]
        ka = sb.xn[1][:, par * 4096: par * 4096 + S_]
        va = sb.xn[0][:, par * 4096: par * 4096 + S_]
        P.dma("sync", qa, att["Qd"].ap()[h, :, :], f"AQ{par}")
        for rk in range(2):
            P.dma("sync", ka[:, rk * TC:(rk + 1) * TC],
                  att["CCKo"].ap()[rk * D + h * 128:(rk * D) + (h + 1) * 128, :], f"AQ{par}")
            src = att["CCVo"].ap()[rk * NHEADS * TC + h * TC: rk * NHEADS * TC + (h + 1) * TC, :]
            v_ld = P.dma("sync", va[:, rk * TC:(rk + 1) * TC].rearrange("p (j d) -> p j d", d=128),
                         src.rearrange("(j p) d -> p j d", p=128), f"AQ{par}")
        if acfg["variant"] == 0:
            P.wait("vector", f"AQ{par}", v_ld)
            P.op("vector", lambda e: e.memset(r(5)[0:1, 0:2], 0.0))
            for src_t, n_ch, slot in ((qa, TC // TT, 0), (ka, S_ // TT, 1)):
                for j in range(n_ch):
                    stt_ = sb.stage[j % 2][:]
                    lastq = state.get(("nrm_pe", j % 2))
                    if lastq is not None:
                        P.wait("scalar", "PE", lastq)
                    P.wait("scalar", f"AQ{par}", v_ld)
                    v_sq = P.op("scalar", lambda e, o=stt_, i=src_t[:, j * TT:(j + 1) * TT]:
                                e.activation(o, i, AF.Square))
                    P.wait("tensor", "ACT", v_sq)
                    P.wait("tensor", "DVE", P.cnt["DVE"])
                    v_n = P.op("tensor", lambda e, o=sb.ps[6][0:1, :], l=sb.ones_col[:], r2=stt_:
                               e.matmul(o, l, r2, start=True, stop=True), inc="PE")
                    state[("nrm_pe", j % 2)] = v_n
                    P.wait("vector", "PE", v_n)
                    P.op("vector", lambda e: e.reduce_max(r(4)[0:1, 0:1], sb.ps[6][0:1, :], axis=AX.X))
                    P.op("vector", lambda e, sl=slot: e.tensor_max(r(5)[0:1, sl:sl + 1],
                                                                   r(5)[0:1, sl:sl + 1],
                                                                   r(4)[0:1, 0:1]))
            P.op("vector", lambda e: e.tensor_add(r(4)[0:1, 0:1], r(5)[0:1, 0:1], r(5)[0:1, 1:2]))
            v_mh = P.op("vector", lambda e: e.tensor_scalar_mul(r(4)[0:1, 0:1], r(4)[0:1, 0:1], -0.5))
            P.wait("tensor", "DVE", v_mh)
            v_bb = P.op("tensor", lambda e: e.matmul(sb.ps[5][:, 0:1], sb.ones_row[0:1, :],
                                                     r(4)[0:1, 0:1], start=True, stop=True), inc="PE")
            P.wait("vector", "PE", v_bb)
            v_cb = P.op("vector", lambda e: e.tensor_copy(sb.colbuf[:, 0:1], sb.ps[5][:, 0:1]))
            state["bias_dve"] = v_cb
        P.wait("tensor", f"AQ{par}", v_ld)
        for qc in range(nqc):
            _, kbs = qc_info[(parity_sym, qc)]
            nkb_q = len(kbs)
            qp = qc % 2
            ob = sb.ps[2 + qp]
            db = sb.ps[4] if qp == 0 else sb.ps[6]
            bb = sb.ps[5] if qp == 0 else sb.ps[7]
            rr = r(3) if qp == 0 else r(2)

            def emit_st(i):
                g_i = kbs[i][0]
                stb = sb.ps[i % 2]
                lastb = state.get(("st_act", i % 2))
                if lastb is not None:
                    P.wait("tensor", lastb[0], lastb[1])
                return P.op("tensor",
                            lambda e, o=stb[:], l=ka[:, g_i * 128:(g_i + 1) * 128],
                            r2=qa[:, qc * TT:(qc + 1) * TT]:
                            e.matmul(o, l, r2, start=True, stop=True), inc="PE")

            v_st = {0: emit_st(0)}
            for idx, (g, mask_idx) in enumerate(kbs):
                stb = sb.ps[idx % 2]
                pst = sb.stage[2 + idx % 2][:]
                lastp = state.get(("pst_pe", idx % 2))
                P.wait("scalar", "PE", v_st[idx])
                if lastp is not None:
                    P.wait("scalar", "PE", lastp)
                if acfg["variant"] == 0:
                    P.wait("scalar", "DVE", state["bias_dve"])
                    v_e = P.op("scalar", lambda e, o=pst, i=stb[:]:
                               e.activation(o, i, AF.Exp, bias=sb.colbuf[:, 0:1]))
                    v_ready = ("ACT", v_e)
                    state[("st_act", idx % 2)] = v_ready
                else:
                    if idx % 2 == 0:
                        P.wait("scalar", "DVE", state.get("o_tmp_dve", 0))
                    tq = sb.tmp32[idx % 2][:]
                    v_s2 = P.op("scalar", lambda e, o=tq, i=stb[:]: e.activation(o, i, AF.Square))
                    state[("st_act", idx % 2)] = ("ACT", v_s2)
                    P.wait("gpsimd", "ACT", v_s2)
                    v_e = P.op("gpsimd", lambda e, o=pst, a=tq: e.tensor_mul(o, a, a))
                    v_ready = ("POOL", v_e)
                if mask_idx is not None:
                    if v_ready[0] != "POOL":
                        P.wait("gpsimd", v_ready[0], v_ready[1])
                    v_m = P.op("gpsimd", lambda e, o=pst, m=sb.msk[:, mask_idx * TT:(mask_idx + 1) * TT]:
                               e.tensor_mul(o, o, m))
                    v_ready = ("POOL", v_m)
                if idx + 1 < nkb_q:
                    v_st[idx + 1] = emit_st(idx + 1)
                P.wait("tensor", v_ready[0], v_ready[1])
                if idx == 0:
                    P.wait("tensor", "DVE", state.get(("den_read", qp), 0))
                    P.wait("tensor", "DVE", state.get(("o_evac", qp), 0))
                P.op("tensor", lambda e, o=db[0:1, :], l=sb.ones_col[:], r2=pst,
                     s=(idx == 0), t=(idx == nkb_q - 1): e.matmul(o, l, r2, start=s, stop=t))
                v_av = P.op("tensor",
                            lambda e, o=ob[:], l=va[:, g * 128:(g + 1) * 128], r2=pst,
                            s=(idx == 0), t=(idx == nkb_q - 1):
                            e.matmul(o, l, r2, start=s, stop=t), inc="PE")
                state[("pst_pe", idx % 2)] = v_av
            P.wait("vector", "PE", v_av)
            v_rc = P.op("vector", lambda e, o=rr, i=db[0:1, :]: e.reciprocal(o, i))
            state[("den_read", qp)] = v_rc
            P.wait("tensor", "DVE", v_rc)
            P.wait("tensor", "DVE", state.get(("bc_read", qp), 0))
            v_b2 = P.op("tensor", lambda e, o=bb[:, 0:TT], i=rr: e.matmul(o, sb.ones_row[0:1, :],
                                                                          i, start=True, stop=True),
                        inc="PE")
            o_sl = sb.hid[:, h * TC + qc * TT: h * TC + (qc + 1) * TT]
            P.wait("vector", "PE", v_b2)
            v_cp = P.op("vector", lambda e, i=ob[:]: e.tensor_copy(sb.tmp32[0][:], i))
            v_o = P.op("vector", lambda e, o=o_sl, b2=bb[:, 0:TT]: e.tensor_mul(o, sb.tmp32[0][:], b2))
            state["o_tmp_dve"] = v_o
            state[("o_evac", qp)] = v_cp
            state[("bc_read", qp)] = v_o
            state["o_done"] = v_o
        state[("kv_pe", par)] = P.cnt["PE"]
    return state["o_done"]


def stage_outproj_tile(P, sb, att, X_in, X_out, tt, acfg, state, TC):
    tsl = slice(tt * TT, (tt + 1) * TT)
    gate = state.get("op_gate")
    if gate is not None:
        P.wait("sync", gate[0], gate[1])
    xin_ap = X_in.ap()[:, :, tsl].rearrange("c p t -> p c t")
    v_x = P.dma("sync", sb.xt[:], xin_ap, "XL")
    v_o = state["o_done"]
    for dg in range(8):
        bpar = dg % 4
        w, wp = bpar // 2, bpar % 2
        buf = sb.wup[w][wp]
        last = state["wbuf_last_pe"].get(("o", bpar))
        if last is not None:
            P.wait("sync", "PE", last)
        src = att["Wo16"].ap().rearrange("(k p) ee -> p k ee", p=128)[:, :, dg * 256:(dg + 1) * 256]
        v_w = None
        for kg in range(4):
            v_w = P.dma("sync", buf[:].rearrange("p (k ee) -> p k ee", ee=256)[:, kg * 4:(kg + 1) * 4, :],
                        src[:, kg * 4:(kg + 1) * 4, :], f"WO{bpar}")
        for sub in range(2):
            dm = dg * 2 + sub
            bank = sb.ps[dm % 2]
            lastb = state["pso_last"].get(dm % 2)
            if lastb is not None:
                P.wait("tensor", "DVE", lastb)
            if sub == 0:
                P.wait("tensor", f"WO{bpar}", v_w)
                P.wait("tensor", "DVE", v_o)
            for hc in range(NHEADS):
                lhsT = buf[:, hc * 256 + sub * 128: hc * 256 + sub * 128 + 128]
                rhs = sb.hid[:, hc * TC + tt * TT: hc * TC + (tt + 1) * TT]
                vv = P.op("tensor",
                          lambda e, o=bank[:], l=lhsT, r2=rhs, s=(hc == 0), t=(hc == NHEADS - 1):
                          e.matmul(o, l, r2, start=s, stop=t),
                          inc="PE" if hc == NHEADS - 1 else None)
            if sub == 1:
                state["wbuf_last_pe"][("o", bpar)] = vv
            xc = sb.xt[:, dm * TT:(dm + 1) * TT]
            P.wait("vector", "PE", vv)
            P.wait("vector", "XL", v_x)
            v_ev = P.op("vector",
                        lambda e, o=xc, i=bank[:]:
                        e.scalar_tensor_tensor(o, i, 1.0, o, ALU.mult, ALU.add))
            state["pso_last"][dm % 2] = v_ev
    v_evac = P.cnt["DVE"]
    P.wait("gpsimd", "DVE", v_evac)
    xout_ap = X_out.ap()[:, :, tsl].rearrange("c p t -> p c t")
    v_store = P.dma("gpsimd", xout_ap, sb.xt[:], "XS")
    state["op_gate"] = ("XS", v_store)
    return v_store


def build_attn_launch(variant, pairs=None, S_=S):
    TC = S_ // 2
    ntt = TC // TT
    nc = bass.Bass("TRN2", target_bir_lowering=False)
    X_in = nc.dram_tensor("x_in", [DC, 128, TC], F32, kind="ExternalInput")
    X_out = nc.dram_tensor("x_out", [DC, 128, TC], F32, kind="ExternalOutput")
    WqkvT = nc.dram_tensor("wqkvt", [D, 3 * D], F32, kind="ExternalInput")
    WoT = nc.dram_tensor("wot", [D, D], F32, kind="ExternalInput")
    MSK = nc.dram_tensor("masks", [128, 4 * TT], BF16, kind="ExternalInput")
    EYE = nc.dram_tensor("eyec", [128, 128], BF16, kind="ExternalInput")
    SWP = nc.dram_tensor("swapc", [128, 128], F32, kind="ExternalInput")
    att = {}
    sb = Pool(nc)
    pro_list = [(sb.msk[:], MSK), (sb.eye[:], EYE)]
    if variant == 1:
        LNC = nc.dram_tensor("lncolc", [128, 2], F32, kind="ExternalInput")
        att["ropec"] = nc.dram_tensor("ropec", [128, TC], F32, kind="ExternalInput")
        att["ropes"] = nc.dram_tensor("ropes", [128, TC], F32, kind="ExternalInput")
        pro_list.append((sb.lncol[:], LNC))
        pro_list.append((sb.swp[:], SWP))
    att["Wq16"] = nc.dram_tensor("wq16", [D, 3 * D], BF16)
    att["Wo16"] = nc.dram_tensor("wo16", [D, D], BF16)
    att["Qd"] = nc.dram_tensor("qd", [NHEADS, 128, TC], BF16)
    att["CCKi"] = nc.dram_tensor("ccki", [D, TC], BF16)
    att["CCKo"] = nc.dram_tensor("ccko", [2 * D, TC], BF16)
    att["CCVi"] = nc.dram_tensor("ccvi", [NHEADS * TC, 128], BF16)
    att["CCVo"] = nc.dram_tensor("ccvo", [2 * NHEADS * TC, 128], BF16)

    acfg = {"variant": variant, "qk_scale": float(128.0 ** -0.25)}
    P = Prog(nc)
    pro = stage_prologue(P, sb, pro_list)
    state = fresh_state()
    stage_cast_weight(P, sb, WqkvT, att["Wq16"], D, 3 * D, state)
    stage_cast_weight(P, sb, WoT, att["Wo16"], D, D, state)
    for s in ("CS0", "CS1"):
        P.wait("sync", s, P.cnt.get(s, 0))
    P.wait("tensor", "LW", pro["consts"])
    P.wait("tensor", "DVE", pro["ones"])
    P.wait("scalar", "LW", pro["consts"])
    emit_attn_sublayer(P, sb, att, acfg, X_in, X_out, TC, S_, pairs, state)
    P.wait("sync", "XS", P.cnt["XS"])
    P.finish()
    return nc


def fresh_state():
    return {"wbuf_last_pe": {}, "wbuf_ready": {}, "psq_last": {}, "psgu_last": {},
            "stage_sg": {}, "tr_last": {}, "pso_last": {}, "wdn_last_pe": {}}


def soft_barrier(P):
    tot = {k: v for k, v in P.cnt.items() if v > 0}
    for eng in ENGS:
        for k, v in tot.items():
            P.wait(eng, k, v)


def emit_attn_sublayer(P, sb, att, acfg, X_in, X_out, TC, S_, pairs, state):
    import copy
    ntt = TC // TT
    gate = state.get("x_gate")
    for tt in range(ntt):
        par = tt % 2
        ln = stage_ln_tile(P, sb, X_in, tt, par, gate_prev=gate)
        stage_qkv_tile(P, sb, att, tt, par, ln["xn_done"], acfg, state, TC)
        gate = ("DVE", P.cnt["DVE"])
    if pairs is None:
        pairs = [[2 * i, 2 * i + 1] for i in range(NCORES // 2)]
    v_cc = stage_kv_gather(P, att, pairs)
    P.begin_switch()
    snap = copy.deepcopy(state)
    out_states = []
    for par_body in (0, 1):
        P.begin_body()
        st2 = copy.deepcopy(snap)
        stage_attn_core(P, sb, att, acfg, st2, TC, S_, par_body, v_cc)
        out_states.append(st2)
        P.end_body()
    P.end_switch()
    state.update({k: v for k, v in out_states[0].items()})
    state["op_gate"] = ("DVE", P.cnt["DVE"])
    v_store = None
    for tt in range(ntt):
        v_store = stage_outproj_tile(P, sb, att, X_in, X_out, tt, acfg, state, TC)
    state["x_gate"] = ("XS", v_store)


def emit_ffn_sublayer(P, sb, X_in, X_out, wcfg, state, ntt=NTT):
    gate = state.get("x_gate")
    v_store = None
    for tt in range(ntt):
        par = tt % 2
        ln = stage_ln_tile(P, sb, X_in, tt, par, gate_prev=gate)
        v_store = stage_ffn_tile(P, sb, X_out, tt, par, ln["xn_done"], wcfg, state)
        gate = ("XS", v_store)
    state["x_gate"] = gate


def build_fused_launch(S_=S, pairs=None):
    TC = S_ // 2
    nc = bass.Bass("TRN2", target_bir_lowering=False)
    X_in = nc.dram_tensor("x_in", [DC, 128, TC], F32, kind="ExternalInput")
    X_out = nc.dram_tensor("x_out", [DC, 128, TC], F32, kind="ExternalOutput")
    XA = nc.dram_tensor("xa", [DC, 128, TC], F32)
    XB = nc.dram_tensor("xb", [DC, 128, TC], F32)
    W1T = nc.dram_tensor("w1t", [D, H0], F32, kind="ExternalInput")
    W3T = nc.dram_tensor("w3t", [D, H0], F32, kind="ExternalInput")
    W2T = nc.dram_tensor("w2t", [H0, D], F32, kind="ExternalInput")
    WQ0 = nc.dram_tensor("wqkvt0", [D, 3 * D], F32, kind="ExternalInput")
    WO0 = nc.dram_tensor("wot0", [D, D], F32, kind="ExternalInput")
    GT = nc.dram_tensor("gt", [D, H1], F32, kind="ExternalInput")
    UT = nc.dram_tensor("ut", [D, H1], F32, kind="ExternalInput")
    DT = nc.dram_tensor("dt", [H1, D], F32, kind="ExternalInput")
    WQ1 = nc.dram_tensor("wqkvt1", [D, 3 * D], F32, kind="ExternalInput")
    WO1 = nc.dram_tensor("wot1", [D, D], F32, kind="ExternalInput")
    SCL = nc.dram_tensor("scl", [128, 3 * 64], F32, kind="ExternalInput")
    LNC = nc.dram_tensor("lncolc", [128, 2], F32, kind="ExternalInput")
    ROPEC = nc.dram_tensor("ropec", [128, TC], F32, kind="ExternalInput")
    ROPES = nc.dram_tensor("ropes", [128, TC], F32, kind="ExternalInput")
    MSK = nc.dram_tensor("masks", [128, 4 * TT], BF16, kind="ExternalInput")
    EYE = nc.dram_tensor("eyec", [128, 128], BF16, kind="ExternalInput")
    SWP = nc.dram_tensor("swapc", [128, 128], F32, kind="ExternalInput")
    H128c = nc.dram_tensor("h128c", [128, 128], BF16, kind="ExternalInput")
    H64c = nc.dram_tensor("h64c", [128, 128], BF16, kind="ExternalInput")
    W1T16 = nc.dram_tensor("w1t16", [D, H0], BF16)
    W3T16 = nc.dram_tensor("w3t16", [D, H0], BF16)
    W2A = nc.dram_tensor("w2a", [H0, D], BF16)
    W2pp = nc.dram_tensor("w2pp", [H0, D], BF16)
    GT16 = nc.dram_tensor("gt16", [D, H1], BF16)
    UT16 = nc.dram_tensor("ut16", [D, H1], BF16)
    DT16 = nc.dram_tensor("dt16", [H1, D], BF16)
    WQ016 = nc.dram_tensor("wq016", [D, 3 * D], BF16)
    WO016 = nc.dram_tensor("wo016", [D, D], BF16)
    WQ116 = nc.dram_tensor("wq116", [D, 3 * D], BF16)
    WO116 = nc.dram_tensor("wo116", [D, D], BF16)
    att_common = {
        "Qd": nc.dram_tensor("qd", [NHEADS, 128, TC], BF16),
        "CCKi": nc.dram_tensor("ccki", [D, TC], BF16),
        "CCKo": nc.dram_tensor("ccko", [2 * D, TC], BF16),
        "CCVi": nc.dram_tensor("ccvi", [NHEADS * TC, 128], BF16),
        "CCVo": nc.dram_tensor("ccvo", [2 * NHEADS * TC, 128], BF16),
    }
    att0 = dict(att_common, Wq16=WQ016, Wo16=WO016)
    att1 = dict(att_common, Wq16=WQ116, Wo16=WO116, ropec=ROPEC, ropes=ROPES)

    sb = Pool(nc)
    P = Prog(nc)
    pro = stage_prologue(P, sb, [(sb.eye[:], EYE), (sb.swp[:], SWP),
                                 (sb.lncol[:], LNC), (sb.scale_cols[:], SCL),
                                 (sb.msk[:, 0:128], H128c), (sb.msk[:, 128:256], H64c)])
    state = fresh_state()
    P.wait("tensor", "LW", pro["consts"])
    P.wait("tensor", "DVE", pro["ones"])
    P.wait("scalar", "LW", pro["consts"])
    stage_cast_weight(P, sb, W1T, W1T16, D, H0, state)
    stage_cast_weight(P, sb, W3T, W3T16, D, H0, state)
    stage_w2_transform(P, sb, W2T, W2A, W2pp, sb.msk[:, 0:128], sb.msk[:, 128:256], state)
    soft_barrier(P)
    v_msk = P.dma("sync", sb.msk[:], MSK.ap(), "LW")
    P.wait("tensor", "LW", v_msk)
    P.wait("scalar", "LW", v_msk)
    P.wait("vector", "LW", v_msk)
    qstag = [(sb.qbuf[:, 0:2048].bitcast(BF16) if False else sb.qbuf[:, 0:2048], 2048),
             (sb.qbuf[:, 2048:4096], 2048)]
    wstag = [(sb.wdn[0][:], 1024), (sb.wdn[1][:], 1024)]
    deferred_casts = {
        0: [(WQ0, WQ016, D, 3 * D, qstag), (WO0, WO016, D, D, qstag)],
        1: [(GT, GT16, D, H1, wstag), (UT, UT16, D, H1, wstag),
            (DT, DT16, H1, D, wstag), (WQ1, WQ116, D, 3 * D, wstag),
            (WO1, WO116, D, D, wstag)],
    }

    acfg0 = {"variant": 0, "qk_scale": float(128.0 ** -0.25)}
    acfg1 = {"variant": 1, "qk_scale": float(128.0 ** -0.25)}
    wcfg0 = dict(Wg=W1T16, Wu=W3T16, Wdn=W2pp, HC=H0C,
                 g_scale=None, u_scale=None, d_scale=None)
    wcfg1 = dict(Wg=GT16, Wu=UT16, Wdn=DT16, HC=H1C,
                 g_scale=lambda c: sb.scale_cols[:, c:c + 1],
                 u_scale=lambda c: sb.scale_cols[:, 64 + c:64 + c + 1],
                 d_scale=lambda c: sb.scale_cols[:, 128 + c:128 + c + 1])

    chain = [X_in, XA, XB, XA, XB, XA, X_out]
    sublayers = [
        ("ffn", wcfg0, acfg0), ("attn", att0, acfg0), ("ffn", wcfg0, acfg0),
        ("ffn", wcfg1, acfg1), ("attn", att1, acfg1), ("ffn", wcfg1, acfg1),
    ]
    for i, (kind, cfg, acfg) in enumerate(sublayers):
        if i > 0:
            soft_barrier(P)
        st = fresh_state()
        if kind == "ffn":
            emit_ffn_sublayer(P, sb, chain[i], chain[i + 1], cfg, st, ntt=TC // TT)
        else:
            emit_attn_sublayer(P, sb, cfg, acfg, chain[i], chain[i + 1], TC, S_, pairs, st)
        for args in deferred_casts.pop(i, []):
            s_f32, s_bf16, rr_, cc_, stg = args
            stage_cast_weight(P, sb, s_f32, s_bf16, rr_, cc_, st, staging=stg)
    P.wait("sync", "XS", P.cnt["XS"])
    P.finish()
    return nc


def core_positions(core, S_=S):
    BLK_ = S_ // 4
    blocks = ZIG[core % 2]
    return np.concatenate([np.arange(bb * BLK_, (bb + 1) * BLK_) for bb in blocks])


def host_rope(pos):
    d = np.arange(0, HD, 2, dtype=np.float32) / np.float32(HD)
    inv = (np.float32(1.0) / (np.float32(10000.0) ** d)).astype(np.float32)
    t = np.outer(inv, pos.astype(np.float32)).astype(np.float32)
    return np.cos(t).astype(np.float32), np.sin(t).astype(np.float32)


def host_rope_tables(pos):
    c, s = host_rope(pos)
    C2 = np.vstack([c, c]).astype(np.float32)
    S2 = np.vstack([-s, s]).astype(np.float32)
    return C2, S2


def host_swap():
    m = np.zeros((128, 128), np.float32)
    for p in range(128):
        m[(p + 64) % 128, p] = 1.0
    return m


def host_masks():
    m = np.zeros((128, 4 * TT), np.float32)
    for i in range(4):
        for p in range(128):
            j0 = p + 128 * i
            if j0 < TT:
                m[p, i * TT + j0:(i + 1) * TT] = 1.0
    return m.astype(ml_dtypes.bfloat16)


def hadamard(n):
    h = np.array([[1.0]], dtype=np.float32)
    while h.shape[0] < n:
        h = np.block([[h, h], [h, -h]])
    return h


def host_constants():
    h128 = hadamard(128).astype(ml_dtypes.bfloat16)
    h64 = hadamard(64)
    h64blk = np.zeros((128, 128), np.float32)
    h64blk[:64, :64] = h64
    h64blk[64:, 64:] = h64
    return {"h128c": h128, "h64c": h64blk.astype(ml_dtypes.bfloat16)}


def shard_x(x):
    out = []
    for core in range(NCORES):
        b = core // 2
        blocks = ZIG[core % 2]
        toks = np.concatenate([x[b, bb * BLK:(bb + 1) * BLK, :] for bb in blocks], axis=0)
        fm = np.ascontiguousarray(toks.T.reshape(DC, 128, TCORE))
        out.append(fm)
    return out


def unshard_x(shards):
    x = np.zeros((B, S, D), np.float32)
    for core in range(NCORES):
        b = core // 2
        blocks = ZIG[core % 2]
        toks = shards[core].reshape(D, TCORE).T
        for i, bb in enumerate(blocks):
            x[b, bb * BLK:(bb + 1) * BLK, :] = toks[i * BLK:(i + 1) * BLK, :]
    return x


def ffn1_scale_cols(gate_s, up_s, down_s):
    scl = np.zeros((128, 3 * 64), np.float32)
    scl[:, 0:H1C] = gate_s.reshape(H1C, 128).T
    scl[:, 64:64 + H1C] = up_s.reshape(H1C, 128).T
    scl[:, 128:128 + DC] = (down_s * 0.5).reshape(DC, 128).T
    return scl


_CACHE = {}


def _built(key, fn):
    if key not in _CACHE:
        _CACHE[key] = fn()
    return _CACHE[key]


def kernel(**inputs):
    f32 = lambda a: np.ascontiguousarray(np.asarray(a), dtype=np.float32)
    x = f32(inputs["L0_attn_x"])
    shards = shard_x(x)
    consts = host_constants()
    masks = host_masks()
    eye = np.eye(128, dtype=np.float32).astype(ml_dtypes.bfloat16)
    cores = list(range(NCORES))

    w1t = f32(inputs["L0_ffn_W1"]).T.copy()
    w3t = f32(inputs["L0_ffn_W3"]).T.copy()
    w2t = f32(inputs["L0_ffn_W2"]).T.copy()
    wqkvt0 = f32(inputs["L0_attn_W_qkv"]).T.copy()
    wot0 = f32(inputs["L0_attn_W_o"]).T.copy()
    gt = f32(inputs["L1_ffn_gate_w"]).T.copy()
    ut = f32(inputs["L1_ffn_up_w"]).T.copy()
    dt = f32(inputs["L1_ffn_down_w"]).T.copy()
    scl = ffn1_scale_cols(f32(inputs["L1_ffn_gate_s"]),
                          f32(inputs["L1_ffn_up_s"]),
                          f32(inputs["L1_ffn_down_s"]))
    wqkvt1 = f32(inputs["L1_attn_w_qkv"]).copy()
    wot1 = f32(inputs["L1_attn_w_out"]).copy()
    lncol = np.stack([f32(inputs["L1_attn_ln_scale"]),
                      f32(inputs["L1_attn_ln_bias"])], axis=1).copy()
    ropecs = [host_rope_tables(core_positions(c)) for c in cores]
    swapc = host_swap()

    def run(nc, per_core):
        res = run_bass_kernel_spmd(nc, per_core, core_ids=cores)
        return [np.asarray(res.results[c]["x_out"], dtype=np.float32) for c in cores]

    def ffn0(shards):
        nc = _built("ffn0", build_ffn0_launch)
        return run(nc, [{"x_in": shards[c], "w1t": w1t, "w3t": w3t, "w2t": w2t,
                         "h128c": consts["h128c"], "h64c": consts["h64c"]}
                        for c in cores])

    def ffn1(shards):
        nc = _built("ffn1", build_ffn1_launch)
        return run(nc, [{"x_in": shards[c], "gt": gt, "ut": ut, "dt": dt, "scl": scl}
                        for c in cores])

    def attn0(shards):
        nc = _built("attn0", lambda: build_attn_launch(0))
        return run(nc, [{"x_in": shards[c], "wqkvt": wqkvt0, "wot": wot0,
                         "masks": masks, "eyec": eye, "swapc": swapc} for c in cores])

    def attn1(shards):
        nc = _built("attn1", lambda: build_attn_launch(1))
        return run(nc, [{"x_in": shards[c], "wqkvt": wqkvt1, "wot": wot1,
                         "masks": masks, "eyec": eye, "swapc": swapc, "lncolc": lncol,
                         "ropec": ropecs[c][0], "ropes": ropecs[c][1]}
                        for c in cores])

    import os
    if os.environ.get("MK_UNFUSED"):
        shards = ffn0(shards)
        shards = attn0(shards)
        shards = ffn0(shards)
        shards = ffn1(shards)
        shards = attn1(shards)
        shards = ffn1(shards)
        return unshard_x(shards)

    nc = _built("fused", build_fused_launch)
    per_core = []
    for c in cores:
        per_core.append({
            "x_in": shards[c],
            "w1t": w1t, "w3t": w3t, "w2t": w2t,
            "wqkvt0": wqkvt0, "wot0": wot0,
            "gt": gt, "ut": ut, "dt": dt, "scl": scl,
            "wqkvt1": wqkvt1, "wot1": wot1, "lncolc": lncol,
            "ropec": ropecs[c][0], "ropes": ropecs[c][1],
            "masks": masks, "eyec": eye, "swapc": swapc,
            "h128c": consts["h128c"], "h64c": consts["h64c"],
        })
    res = run_bass_kernel_spmd(nc, per_core, core_ids=cores)
    return unshard_x([np.asarray(res.results[c]["x_out"], dtype=np.float32) for c in cores])


# version 07
# speedup vs baseline: 1.0900x; 1.0900x over previous
import numpy as np
import ml_dtypes
import concourse.bass as bass
import concourse.mybir as mybir
from concourse.bass_utils import run_bass_kernel_spmd

F32 = mybir.dt.float32
BF16 = mybir.dt.bfloat16
AF = mybir.ActivationFunctionType
ALU = mybir.AluOpType
AX = mybir.AxisListType

D = 2048
DC = 16
B = 4
S = 4096
NCORES = 8
TCORE = 2048
NHEADS = 16
HD = 128
H0 = 8192; H0C = 64
H1 = 5632; H1C = 44
QKVC = 48
BLK = 1024
TT = 512
NTT = TCORE // TT
LNEPS = 1e-5
ZIG = {0: (0, 3), 1: (1, 2)}
GATHER_BLOCKS = (0, 3, 1, 2)
GATHER_BASE = {0: 0, 3: 1024, 1: 2048, 2: 3072}

ENGS = ("sync", "gpsimd", "tensor", "scalar", "vector")


class Prog:

    def __init__(self, nc):
        self.nc = nc
        self.ops = {e: [] for e in ENGS}
        self.sems = {}
        self.cnt = {}
        self.segments = [("block", self.ops)]
        self._switch = None

    def sem(self, name):
        if name not in self.sems:
            self.sems[name] = self.nc.alloc_semaphore(name)
        if name not in self.cnt:
            self.cnt[name] = 0
        return self.sems[name]

    def wait(self, eng, name, val):
        h = self.sem(name)
        self.ops[eng].append(lambda e, h=h, v=val: e.wait_ge(h, v))

    ENGINE_SEM = {"scalar": "ACT", "vector": "DVE", "gpsimd": "POOL"}

    def op(self, eng, fn, inc=None, n=1, serial=True, dma=False):
        own = None if dma else self.ENGINE_SEM.get(eng)
        if own is not None:
            assert inc is None or inc == own, f"{eng} ops must inc {own}"
            if serial:
                h0 = self.sem(own)
                v0 = self.cnt[own]
                self.ops[eng].append(lambda e, h=h0, v=v0: e.wait_ge(h, v))
            inc = own
            n = 1
        v = None
        h = None
        if inc is not None:
            h = self.sem(inc)
            self.cnt[inc] += n
            v = self.cnt[inc]
        def run(e, fn=fn, h=h, n=n):
            inst = fn(e)
            if h is not None:
                inst.then_inc(h, n)
        self.ops[eng].append(run)
        return v

    def dma(self, eng, out_ap, in_ap, inc):
        return self.op(eng, lambda e, o=out_ap, i=in_ap: e.dma_start(out=o, in_=i),
                       inc=inc, n=16, dma=True)

    def begin_switch(self):
        self._switch = {"snap_cnt": dict(self.cnt), "bodies": [], "ends": []}
        self.segments.append(("switch", self._switch["bodies"]))

    def begin_body(self):
        self.cnt = dict(self._switch["snap_cnt"])
        self.ops = {e: [] for e in ENGS}
        self._switch["bodies"].append(self.ops)

    def end_body(self):
        self._switch["ends"].append(dict(self.cnt))

    def end_switch(self):
        ends = self._switch["ends"]
        for other in ends[1:]:
            assert other == ends[0], f"switch bodies disagree on sem counts:\n{ends[0]}\n{other}"
        self.cnt = ends[0]
        self._switch = None
        self.ops = {e: [] for e in ENGS}
        self.segments.append(("block", self.ops))

    def finish(self):
        nc = self.nc
        handles = {"sync": nc.sync, "gpsimd": nc.gpsimd, "tensor": nc.tensor,
                   "scalar": nc.scalar, "vector": nc.vector}
        for kind, payload in self.segments:
            if kind == "block":
                if not any(payload[e] for e in ENGS):
                    continue
                with nc.Block() as block:
                    for ename in ENGS:
                        getattr(block, ename)(
                            lambda e, fs=tuple(payload[ename]): [f(e) for f in fs])
            else:
                engines = [handles[e] for e in ENGS]
                regs = []
                for eng in engines:
                    pid = eng.partition_id()
                    regs.append(eng.compute_val(pid % 2))
                for i in nc.Switch(engines=engines, index=regs, n=len(payload)):
                    body = payload[i]
                    for ename in ENGS:
                        e = handles[ename]
                        for f in body[ename]:
                            f(e)
        return nc


class Pool:

    def __init__(self, nc):
        self.nc = nc
        self.xt = nc.alloc_sbuf_tensor("xt", [128, DC * TT], F32)
        self.xn = [nc.alloc_sbuf_tensor(f"xn{i}", [128, DC * TT], BF16) for i in range(2)]
        self.xbf = [nc.alloc_sbuf_tensor(f"xbf{i}", [128, TT], BF16) for i in range(2)]
        self.xsq = [nc.alloc_sbuf_tensor(f"xsq{i}", [128, TT], BF16) for i in range(2)]
        self.rows = nc.alloc_sbuf_tensor("rows", [1, 6 * TT], F32)
        self.tmp32 = [nc.alloc_sbuf_tensor(f"tmp32_{i}", [128, TT], F32) for i in range(2)]
        self.hid = nc.alloc_sbuf_tensor("hid", [128, H0C * TT], BF16)
        self.wup = [[nc.alloc_sbuf_tensor(f"wup{w}_{p}", [128, DC * 256], BF16)
                     for p in range(2)] for w in range(2)]
        self.wdn = [nc.alloc_sbuf_tensor(f"wdn{p}", [128, 1024], BF16) for p in range(2)]
        self.stage = [nc.alloc_sbuf_tensor(f"stage{i}", [128, TT], BF16) for i in range(4)]
        self.ones_col = nc.alloc_sbuf_tensor("ones_col", [128, 1], BF16)
        self.ones_row = nc.alloc_sbuf_tensor("ones_row", [1, 128], F32)
        self.scale_cols = nc.alloc_sbuf_tensor("scale_cols", [128, 3 * 64], F32)
        self.qbuf = nc.alloc_sbuf_tensor("qbuf", [128, 2 * 2048], BF16)
        self.msk = nc.alloc_sbuf_tensor("msk", [128, 4 * TT], BF16)
        self.eye = nc.alloc_sbuf_tensor("eye", [128, 128], BF16)
        self.qk32 = self.tmp32
        self.qk2 = nc.alloc_sbuf_tensor("qk2", [128, 2 * TT], F32)
        self.swp = nc.alloc_sbuf_tensor("swp", [128, 128], F32)
        self.colbuf = nc.alloc_sbuf_tensor("colbuf", [128, 4], F32)
        self.lncol = nc.alloc_sbuf_tensor("lncol", [128, 2], F32)
        self.ps = [nc.alloc_psum_tensor(f"ps{i}", [128, 512], F32) for i in range(8)]


def stage_ln_tile(P, sb, X_in, tt, par, gate_prev=None):
    tsl = slice(tt * TT, (tt + 1) * TT)
    if gate_prev is not None:
        P.wait("sync", gate_prev[0], gate_prev[1])
    xin_ap = X_in.ap()[:, :, tsl].rearrange("c p t -> p c t")
    v_xload = P.dma("sync", sb.xt[:], xin_ap, "XL")

    P.wait("scalar", "XL", v_xload)
    pe_after_cast = {}
    for c in range(DC):
        pp = c % 2
        if c >= 2:
            P.wait("scalar", "PE", pe_after_cast[c - 2])
        xc = sb.xt[:, c * TT:(c + 1) * TT]
        P.op("scalar", lambda e, o=sb.xbf[pp][:], i=xc: e.activation(o, i, AF.Copy))
        v_act = P.op("scalar", lambda e, o=sb.xsq[pp][:], i=xc: e.activation(o, i, AF.Square),
                     inc="ACT")
        P.wait("tensor", "ACT", v_act)
        P.op("tensor", lambda e, o=sb.ps[6][0:1, :], l=sb.ones_col[:], r=sb.xbf[pp][:],
             s=(c == 0), t=(c == DC - 1): e.matmul(o, l, r, start=s, stop=t))
        pe_after_cast[c] = P.op(
            "tensor", lambda e, o=sb.ps[7][0:1, :], l=sb.ones_col[:], r=sb.xsq[pp][:],
            s=(c == 0), t=(c == DC - 1): e.matmul(o, l, r, start=s, stop=t),
            inc="PE")
    v_stats = pe_after_cast[DC - 1]

    r = lambda i: sb.rows[0:1, i * TT:(i + 1) * TT]
    P.wait("vector", "PE", v_stats)
    P.op("vector", lambda e: e.tensor_scalar_mul(r(0), sb.ps[6][0:1, :], 1.0 / D))
    P.op("vector", lambda e: e.tensor_scalar_mul(r(1), sb.ps[7][0:1, :], 1.0 / D))
    P.op("vector", lambda e: e.tensor_mul(r(2), r(0), r(0)))
    P.op("vector", lambda e: e.tensor_sub(r(2), r(1), r(2)))
    P.op("vector", lambda e: e.tensor_scalar_add(r(2), r(2), LNEPS))
    v_dve = P.op("vector", lambda e: e.reciprocal(r(3), r(2)), inc="DVE")
    P.wait("scalar", "DVE", v_dve)
    P.wait("scalar", "PE", P.cnt["PE"])
    v_sqrt = P.op("scalar", lambda e: e.activation(r(4), r(3), AF.Sqrt), inc="ACT")
    P.wait("vector", "ACT", v_sqrt)
    v_mr = P.op("vector", lambda e: e.tensor_mul(r(5), r(0), r(4)), inc="DVE")

    P.wait("tensor", "DVE", v_mr)
    P.op("tensor", lambda e: e.matmul(sb.ps[6][:, 0:TT], sb.ones_row[0:1, :], r(4),
                                      start=True, stop=True))
    v_bc = P.op("tensor", lambda e: e.matmul(sb.ps[7][:, 0:TT], sb.ones_row[0:1, :], r(5),
                                             start=True, stop=True), inc="PE")

    P.wait("vector", "PE", v_bc)
    v_apply = None
    for c in range(DC):
        xc = sb.xt[:, c * TT:(c + 1) * TT]
        xnc = sb.xn[par][:, c * TT:(c + 1) * TT]
        tp = sb.tmp32[c % 2][:]
        P.op("vector", lambda e, tp=tp, xc=xc: e.tensor_mul(tp, xc, sb.ps[6][:, 0:TT]))
        v_apply = P.op("vector", lambda e, o=xnc, tp=tp: e.tensor_sub(o, tp, sb.ps[7][:, 0:TT]),
                       inc="DVE")
    return {"xload": v_xload, "xn_done": v_apply}


def stage_ffn_tile(P, sb, X_out, tt, par, xn_done, wcfg, state):
    HC = wcfg["HC"]
    NHG = HC // 2
    tsl = slice(tt * TT, (tt + 1) * TT)
    P.wait("tensor", "DVE", xn_done)

    for hg in range(NHG):
        wpar = hg % 2
        for w in range(2):
            Wd = wcfg["Wg"] if w == 0 else wcfg["Wu"]
            buf = sb.wup[w][wpar]
            last = state["wbuf_last_pe"].get((w, wpar))
            if last is not None:
                P.wait("sync", "PE", last)
            src = Wd.ap().rearrange("(k p) h -> p k h", p=128)[:, :, hg * 256:(hg + 1) * 256]
            v_w = None
            for kg in range(4):
                v_w = P.dma("sync", buf[:].rearrange("p (k h) -> p k h", h=256)[:, kg * 4:(kg + 1) * 4, :],
                            src[:, kg * 4:(kg + 1) * 4, :], f"WU{wpar}")
            state["wbuf_ready"][(w, wpar)] = v_w
        for hc in range(2):
            c = hg * 2 + hc
            bg = sb.ps[(c % 2) * 2 + 0]
            bu = sb.ps[(c % 2) * 2 + 1]
            lastg = state["psgu_last"].get(c % 2)
            if lastg is not None:
                P.wait("tensor", "DVE", lastg)
            if hc == 0:
                P.wait("tensor", f"WU{wpar}", state["wbuf_ready"][(1, wpar)])
            pe_done = {}
            for w in range(2):
                bank = bg if w == 0 else bu
                buf = sb.wup[w][wpar]
                for k in range(DC):
                    lhsT = buf[:, k * 256 + hc * 128: k * 256 + hc * 128 + 128]
                    rhs = sb.xn[par][:, k * TT:(k + 1) * TT]
                    vv = P.op("tensor",
                              lambda e, o=bank[:], l=lhsT, r=rhs, s=(k == 0), t=(k == DC - 1):
                              e.matmul(o, l, r, start=s, stop=t),
                              inc="PE" if k == DC - 1 else None)
                pe_done[w] = vv
            if hc == 1:
                state["wbuf_last_pe"][(0, wpar)] = pe_done[0]
                state["wbuf_last_pe"][(1, wpar)] = pe_done[1]
            tg = sb.tmp32[c % 2][:]
            P.wait("scalar", "PE", pe_done[1])
            gs = wcfg["g_scale"](c) if wcfg["g_scale"] else None
            us = wcfg["u_scale"](c) if wcfg["u_scale"] else None
            hc_ap = sb.hid[:, c * TT:(c + 1) * TT]
            if gs is not None:
                v_silu = P.op("scalar", lambda e, o=tg, i=bg[:], s=gs:
                              e.activation(o, i, AF.Silu, scale=s), inc="ACT")
            else:
                v_silu = P.op("scalar", lambda e, o=tg, i=bg[:]:
                              e.activation(o, i, AF.Silu), inc="ACT")
            if us is not None:
                tu = sb.stage[c % 2][:]
                v_act = P.op("scalar", lambda e, o=tu, i=bu[:], s=us:
                             e.activation(o, i, AF.Copy, scale=s), inc="ACT")
                P.wait("vector", "ACT", v_act)
                v_h = P.op("vector", lambda e, o=hc_ap, a=tg, b=tu: e.tensor_mul(o, a, b), inc="DVE")
            else:
                P.wait("vector", "ACT", v_silu)
                v_h = P.op("vector", lambda e, o=hc_ap, a=tg, b=bu[:]: e.tensor_mul(o, a, b), inc="DVE")
            state["psgu_last"][c % 2] = v_h
            state["hid_done"] = v_h

    v_hid = state["hid_done"]
    P.wait("tensor", "DVE", v_hid)
    for dmh in range(2):
        if dmh == 1:
            P.wait("tensor", "DVE", P.cnt["DVE"])
        bank_pe_done = {}
        for f in range(HC):
            wpar = f % 2
            last = state["wdn_last_pe"].get(wpar)
            if last is not None:
                P.wait("sync", "PE", last)
            v_w = P.dma("sync", sb.wdn[wpar][:],
                        wcfg["Wdn"].ap()[f * 128:(f + 1) * 128, dmh * 1024:(dmh + 1) * 1024],
                        f"WD{wpar}")
            P.wait("tensor", f"WD{wpar}", v_w)
            for m8 in range(8):
                lhsT = sb.wdn[wpar][:, m8 * 128:(m8 + 1) * 128]
                rhs = sb.hid[:, f * TT:(f + 1) * TT]
                vv = P.op("tensor",
                          lambda e, o=sb.ps[m8][:], l=lhsT, r=rhs, s=(f == 0), t=(f == HC - 1):
                          e.matmul(o, l, r, start=s, stop=t),
                          inc="PE" if (f == HC - 1 or m8 == 7) else None)
                if f == HC - 1:
                    bank_pe_done[m8] = vv
            state["wdn_last_pe"][wpar] = P.cnt["PE"]
        for m8 in range(8):
            c = dmh * 8 + m8
            xc = sb.xt[:, c * TT:(c + 1) * TT]
            P.wait("vector", "PE", bank_pe_done[m8])
            ds = wcfg["d_scale"](c) if wcfg["d_scale"] else 1.0
            P.op("vector",
                 lambda e, o=xc, i=sb.ps[m8][:], s=ds:
                 e.scalar_tensor_tensor(o, i, s, o, ALU.mult, ALU.add),
                 inc="DVE")
    v_evac = P.cnt["DVE"]
    P.wait("gpsimd", "DVE", v_evac)
    xout_ap = X_out.ap()[:, :, tsl].rearrange("c p t -> p c t")
    v_store = P.dma("gpsimd", xout_ap, sb.xt[:], "XS")
    return v_store


def stage_cast_weight(P, sb, src_f32, dst_bf16, rows, cols, state, staging=None):
    assert rows % 128 == 0
    if staging is None:
        staging = [(sb.hid[:, 0:H0C * TT // 2], H0C * TT // 2),
                   (sb.hid[:, H0C * TT // 2:H0C * TT], H0C * TT // 2)]
    it = 0
    for rc in range(rows // 128):
        c0 = 0
        while c0 < cols:
            par = it % 2
            buf, width = staging[par]
            cw = min(width, cols - c0)
            last = state.get(("cast_last", par))
            if last is not None:
                P.wait("gpsimd", f"CS{par}", last)
            v_in = P.dma("gpsimd", buf[:, 0:cw],
                         src_f32.ap()[rc * 128:(rc + 1) * 128, c0:c0 + cw], "CL")
            P.wait("gpsimd", "CL", v_in)
            v_out = P.dma("gpsimd", dst_bf16.ap()[rc * 128:(rc + 1) * 128, c0:c0 + cw],
                          buf[:, 0:cw], f"CS{par}")
            state[("cast_last", par)] = v_out
            c0 += cw
            it += 1
    return state[("cast_last", (it - 1) % 2)]


def stage_w2_transform(P, sb, W2T_f32, W2A_bf16, W2pp_bf16, h128_sb, h64_sb, state):
    SCALE = 0.5 / float(np.sqrt(8192.0))

    def one_pass(n_iter, lhs_sb, src_ap_of, dst_ap_of, scale):
        for it in range(n_iter):
            par = it % 2
            src = sb.wup[0][par]
            dst = sb.wup[1][par]
            last_pe = state.get(("w2src_pe", par))
            if last_pe is not None:
                P.wait("gpsimd", "PE", last_pe)
            v_in = None
            for sb_slice, dram_ap in src_ap_of(it, src):
                v_in = P.dma("gpsimd", sb_slice, dram_ap, "TL")
            P.wait("tensor", "TL", v_in)
            acts = []
            for nn in range(4):
                bidx = 4 + 2 * par + (nn % 2)
                last_ev = state.get(("w2bank", bidx))
                if last_ev is not None:
                    P.wait("tensor", "ACT", last_ev)
                vv = P.op("tensor",
                          lambda e, o=sb.ps[bidx][:], l=lhs_sb, r=src[:, nn * 512:(nn + 1) * 512]:
                          e.matmul(o, l, r, start=True, stop=True),
                          inc="PE")
                if nn == 3:
                    state[("w2src_pe", par)] = vv
                P.wait("scalar", "PE", vv)
                if nn == 0:
                    last_sg = state.get(("w2dst_sg", par))
                    if last_sg is not None:
                        P.wait("scalar", f"TS{par}", last_sg)
                v_e = P.op("scalar",
                           lambda e, o=dst[:, nn * 512:(nn + 1) * 512], i=sb.ps[bidx][:], s=scale:
                           e.activation(o, i, AF.Copy, scale=s),
                           inc="ACT")
                state[("w2bank", bidx)] = v_e
                acts.append(v_e)
            P.wait("gpsimd", "ACT", acts[-1])
            v_out = None
            for dram_ap, sb_slice in dst_ap_of(it, dst):
                v_out = P.dma("gpsimd", dram_ap, sb_slice, f"TS{par}")
            state[("w2dst_sg", par)] = v_out
        return (P.cnt.get("TS0", 0), P.cnt.get("TS1", 0))

    v_allA = one_pass(
        H0C, h128_sb,
        lambda c, src: [(src[:, 0:2048], W2T_f32.ap()[c * 128:(c + 1) * 128, :])],
        lambda c, dst: [(W2A_bf16.ap()[c * 128:(c + 1) * 128, :], dst[:, 0:2048])],
        SCALE)
    w2a_r = W2A_bf16.ap().rearrange("(a btwo) d -> a btwo d", a=64)
    w2pp_r = W2pp_bf16.ap().rearrange("(a btwo) d -> a btwo d", a=64)
    P.wait("gpsimd", "TS0", v_allA[0])
    P.wait("gpsimd", "TS1", v_allA[1])
    one_pass(
        64, h64_sb,
        lambda bhi, src: [(src[b0 * 64:(b0 + 1) * 64, 0:2048], w2a_r[:, 2 * bhi + b0, :])
                          for b0 in range(2)],
        lambda bhi, dst: [(w2pp_r[:, 2 * bhi + b0, :], dst[b0 * 64:(b0 + 1) * 64, 0:2048])
                          for b0 in range(2)],
        1.0)
    return None


def stage_prologue(P, sb, consts):
    P.op("vector", lambda e: e.memset(sb.ones_col[:], 1.0))
    P.op("vector", lambda e: e.memset(sb.ones_row[:], 1.0), inc="DVE")
    v = None
    for ap, dram in consts:
        v = P.dma("sync", ap, dram.ap(), "LW")
    return {"consts": v, "ones": P.cnt["DVE"]}


def build_ffn0_launch(T_core=TCORE):
    nc = bass.Bass("TRN2", target_bir_lowering=False)
    ntt = T_core // TT
    X_in = nc.dram_tensor("x_in", [DC, 128, T_core], F32, kind="ExternalInput")
    X_out = nc.dram_tensor("x_out", [DC, 128, T_core], F32, kind="ExternalOutput")
    W1T = nc.dram_tensor("w1t", [D, H0], F32, kind="ExternalInput")
    W3T = nc.dram_tensor("w3t", [D, H0], F32, kind="ExternalInput")
    W2T = nc.dram_tensor("w2t", [H0, D], F32, kind="ExternalInput")
    H128c = nc.dram_tensor("h128c", [128, 128], BF16, kind="ExternalInput")
    H64c = nc.dram_tensor("h64c", [128, 128], BF16, kind="ExternalInput")
    W1T16 = nc.dram_tensor("w1t16", [D, H0], BF16)
    W3T16 = nc.dram_tensor("w3t16", [D, H0], BF16)
    W2A = nc.dram_tensor("w2a", [H0, D], BF16)
    W2pp = nc.dram_tensor("w2pp", [H0, D], BF16)

    sb = Pool(nc)
    h128_sb = sb.msk.ap_handle if False else sb.msk
    h128_sb_ap = sb.msk[:, 0:128]
    h64_sb_ap = sb.msk[:, 128:256]

    P = Prog(nc)
    pro = stage_prologue(P, sb, [(h128_sb_ap, H128c), (h64_sb_ap, H64c)])
    state = {"wbuf_last_pe": {}, "wbuf_ready": {}, "psgu_last": {}, "wdn_last_pe": {}}
    P.wait("tensor", "LW", pro["consts"])
    P.wait("tensor", "DVE", pro["ones"])
    stage_w2_transform(P, sb, W2T, W2A, W2pp, h128_sb_ap, h64_sb_ap, state)
    stage_cast_weight(P, sb, W1T, W1T16, D, H0, state)
    stage_cast_weight(P, sb, W3T, W3T16, D, H0, state)
    for s in ("TS0", "TS1", "CS0", "CS1"):
        P.wait("sync", s, P.cnt.get(s, 0))

    wcfg = dict(Wg=W1T16, Wu=W3T16, Wdn=W2pp, HC=H0C,
                g_scale=None, u_scale=None, d_scale=None)
    gate = None
    for tt in range(ntt):
        par = tt % 2
        ln = stage_ln_tile(P, sb, X_in, tt, par, gate_prev=gate)
        v_store = stage_ffn_tile(P, sb, X_out, tt, par, ln["xn_done"], wcfg, state)
        gate = ("XS", v_store)
    P.wait("sync", "XS", P.cnt["XS"])
    P.finish()
    return nc


def build_ffn1_launch(T_core=TCORE):
    nc = bass.Bass("TRN2", target_bir_lowering=False)
    ntt = T_core // TT
    X_in = nc.dram_tensor("x_in", [DC, 128, T_core], F32, kind="ExternalInput")
    X_out = nc.dram_tensor("x_out", [DC, 128, T_core], F32, kind="ExternalOutput")
    GT = nc.dram_tensor("gt", [D, H1], F32, kind="ExternalInput")
    UT = nc.dram_tensor("ut", [D, H1], F32, kind="ExternalInput")
    DT = nc.dram_tensor("dt", [H1, D], F32, kind="ExternalInput")
    SCL = nc.dram_tensor("scl", [128, 3 * 64], F32, kind="ExternalInput")
    GT16 = nc.dram_tensor("gt16", [D, H1], BF16)
    UT16 = nc.dram_tensor("ut16", [D, H1], BF16)
    DT16 = nc.dram_tensor("dt16", [H1, D], BF16)

    sb = Pool(nc)
    P = Prog(nc)
    pro = stage_prologue(P, sb, [(sb.scale_cols[:], SCL)])
    state = {"wbuf_last_pe": {}, "wbuf_ready": {}, "psgu_last": {}, "wdn_last_pe": {}}
    stage_cast_weight(P, sb, GT, GT16, D, H1, state)
    stage_cast_weight(P, sb, UT, UT16, D, H1, state)
    stage_cast_weight(P, sb, DT, DT16, H1, D, state)
    for s in ("CS0", "CS1"):
        P.wait("sync", s, P.cnt.get(s, 0))
    P.wait("scalar", "LW", pro["consts"])
    P.wait("tensor", "DVE", pro["ones"])

    wcfg = dict(Wg=GT16, Wu=UT16, Wdn=DT16, HC=H1C,
                g_scale=lambda c: sb.scale_cols[:, c:c + 1],
                u_scale=lambda c: sb.scale_cols[:, 64 + c:64 + c + 1],
                d_scale=lambda c: sb.scale_cols[:, 128 + c:128 + c + 1])
    gate = None
    for tt in range(ntt):
        par = tt % 2
        ln = stage_ln_tile(P, sb, X_in, tt, par, gate_prev=gate)
        v_store = stage_ffn_tile(P, sb, X_out, tt, par, ln["xn_done"], wcfg, state)
        gate = ("XS", v_store)
    P.wait("sync", "XS", P.cnt["XS"])
    P.finish()
    return nc


def attn_geometry(S_):
    BLK_ = S_ // 4
    TC = S_ // 2
    nkb = S_ // 128
    per = BLK_ // 128
    order = [0, 3, 1, 2]
    gather_abs_base = {}
    for slot, ab in enumerate(order):
        for j in range(per):
            gather_abs_base[slot * per + j] = ab * BLK_ + j * 128
    qpb = BLK_ // TT
    qc_info = {}
    for parity in (0, 1):
        blocks = ZIG[parity]
        for qc in range(TC // TT):
            abs_qbase = blocks[qc // qpb] * BLK_ + (qc % qpb) * TT
            kbs = []
            for g in range(nkb):
                kb = gather_abs_base[g]
                if kb > abs_qbase + TT - 1:
                    continue
                off = kb - abs_qbase
                mask_idx = off // 128 if off >= 0 else None
                kbs.append((g, mask_idx))
            qc_info[(parity, qc)] = (abs_qbase, kbs)
    return TC, BLK_, gather_abs_base, qc_info


def stage_qkv_tile(P, sb, att, tt, par, xn_done, acfg, state, TC):
    tsl = slice(tt * TT, (tt + 1) * TT)
    P.wait("tensor", "DVE", xn_done)
    if acfg["variant"] == 1:
        rv = sb.hid[:, 0:4 * TT].bitcast(F32)
        P.wait("sync", "DVE", state.get("rope_used", 0))
        P.dma("sync", rv[:, 0:TT], att["ropec"].ap()[:, tsl], "RP")
        v_rope = P.dma("sync", rv[:, TT:2 * TT], att["ropes"].ap()[:, tsl], "RP")
        state["rope_v"] = v_rope

    for og in range(QKVC // 2):
        bpar = og % 4
        w, wp = bpar // 2, bpar % 2
        buf = sb.wup[w][wp]
        last = state["wbuf_last_pe"].get(("q", bpar))
        if last is not None:
            P.wait("sync", "PE", last)
        src = att["Wq16"].ap().rearrange("(k p) h -> p k h", p=128)[:, :, og * 256:(og + 1) * 256]
        v_w = None
        for kg in range(4):
            v_w = P.dma("sync", buf[:].rearrange("p (k h) -> p k h", h=256)[:, kg * 4:(kg + 1) * 4, :],
                        src[:, kg * 4:(kg + 1) * 4, :], f"WQ{bpar}")
        for hc in range(2):
            oc = og * 2 + hc
            kind = oc // NHEADS
            h = oc % NHEADS
            bank = sb.ps[oc % 2]
            lastb = state["psq_last"].get(oc % 2)
            if lastb is not None:
                P.wait("tensor", "ACT", lastb)
            if hc == 0:
                P.wait("tensor", f"WQ{bpar}", v_w)
            for k in range(DC):
                lhsT = buf[:, k * 256 + hc * 128: k * 256 + hc * 128 + 128]
                rhs = sb.xn[par][:, k * TT:(k + 1) * TT]
                vv = P.op("tensor",
                          lambda e, o=bank[:], l=lhsT, r=rhs, s=(k == 0), t=(k == DC - 1):
                          e.matmul(o, l, r, start=s, stop=t),
                          inc="PE" if k == DC - 1 else None)
            if hc == 1:
                state["wbuf_last_pe"][("q", bpar)] = vv
            P.wait("scalar", "PE", vv)
            st = sb.stage[oc % 4][:]
            lastst = state["stage_sg"].get(oc % 4)
            if lastst is not None:
                P.wait("scalar", f"SO{oc % 4}", lastst)
            if kind == 2:
                v_c = P.op("scalar", lambda e, o=st, i=bank[:]: e.activation(o, i, AF.Copy))
                tpar = oc % 2
                lasttr = state["tr_last"].get(tpar)
                if lasttr is not None:
                    P.wait("tensor", "DVE", lasttr)
                P.wait("tensor", "ACT", v_c)
                vt = None
                for j in range(4):
                    reg = sb.ps[2][:, tpar * 256 + j * 64: tpar * 256 + (j + 1) * 64].bitcast(BF16)
                    vt = P.op("tensor",
                              lambda e, o=reg, i=st[:, j * 128:(j + 1) * 128]:
                              e.transpose(o, i, sb.eye[:]),
                              inc="PE")
                P.wait("vector", "PE", vt)
                big = sb.ps[2][:, tpar * 256: tpar * 256 + 256].bitcast(BF16)
                v_cp = P.op("vector", lambda e, o=st, i=big: e.tensor_copy(o, i))
                state["tr_last"][tpar] = v_cp
                P.wait("gpsimd", "DVE", v_cp)
                state["psq_last"][oc % 2] = v_c
                dst = att["CCVi"].ap()[h * TC + tt * TT: h * TC + (tt + 1) * TT, :]
                dst = dst.rearrange("(j p) d -> p j d", p=128)
                v_out = P.dma("gpsimd", dst, st.rearrange("p (j d) -> p j d", d=128), f"SO{oc % 4}")
                state["stage_sg"][oc % 4] = v_out
            else:
                if acfg["variant"] == 0:
                    scale = acfg["qk_scale"]
                    v_e = P.op("scalar", lambda e, o=st, i=bank[:], s=scale:
                               e.activation(o, i, AF.Copy, scale=s))
                    state["psq_last"][oc % 2] = v_e
                    P.wait("gpsimd", "ACT", v_e)
                else:
                    v_e, v_cpy = _rope_headln(P, sb, att, bank, st, par, oc % 2, state)
                    state["psq_last"][oc % 2] = v_cpy
                    P.wait("gpsimd", "DVE", v_e)
                if kind == 0:
                    dst = att["Qd"].ap()[h, :, tsl]
                else:
                    dst = att["CCKi"].ap()[h * 128:(h + 1) * 128, tsl]
                v_out = P.dma("gpsimd", dst, st, f"SO{oc % 4}")
                state["stage_sg"][oc % 4] = v_out
    return


def _rope_headln(P, sb, att, bank, st, par, pr, state):
    q32 = sb.tmp32[pr][:]
    r32 = sb.qk2[:, pr * TT:(pr + 1) * TT]
    bs, bb2 = (sb.ps[4], sb.ps[5]) if pr == 0 else (sb.ps[6], sb.ps[7])
    rb = 3 * pr
    r = lambda i: sb.rows[0:1, (rb + i) * TT:(rb + i + 1) * TT]
    rv = sb.hid[:, 0:4 * TT].bitcast(F32)
    cc = rv[:, 0:TT]
    ss = rv[:, TT:2 * TT]
    P.wait("scalar", "RP", state["rope_v"])
    q32g = state.get(("rp_q32", pr))
    if q32g is not None:
        P.wait("scalar", "DVE", q32g[0])
        P.wait("scalar", "PE", q32g[1])
    v_c = P.op("scalar", lambda e, o=q32, i=bank[:]: e.activation(o, i, AF.Copy))
    P.wait("tensor", "ACT", v_c)
    lastr = state.get("rope_ps")
    if lastr is not None:
        P.wait("tensor", "DVE", lastr)
    v_sw = P.op("tensor", lambda e: e.matmul(sb.ps[3][:], sb.swp[:], q32,
                                             start=True, stop=True), inc="PE")
    P.wait("vector", "PE", v_sw)
    r32g = state.get(("rp_r32", pr))
    if r32g is not None:
        P.wait("vector", "ACT", r32g)
    v_r1 = P.op("vector", lambda e: e.tensor_mul(r32, q32, cc))
    v_r2 = P.op("vector", lambda e: e.tensor_mul(q32, sb.ps[3][:], ss))
    v_rope = P.op("vector", lambda e: e.tensor_add(r32, r32, q32))
    state["rope_ps"] = v_r2
    state["rope_used"] = v_rope
    P.wait("scalar", "DVE", v_rope)
    xbg = state.get(("rp_xbf", pr))
    if xbg is not None:
        P.wait("scalar", "PE", xbg)
    v_b = P.op("scalar", lambda e, o=sb.xbf[pr][:], i=r32: e.activation(o, i, AF.Copy))
    v_s = P.op("scalar", lambda e, o=sb.xsq[pr][:], i=r32: e.activation(o, i, AF.Square))
    state[("rp_r32", pr)] = v_s
    P.wait("tensor", "ACT", v_s)
    bkg = state.get(("rp_bank", pr))
    if bkg is not None:
        P.wait("tensor", "DVE", bkg)
    P.op("tensor", lambda e, o=bs[0:1, :]: e.matmul(o, sb.ones_col[:], sb.xbf[pr][:],
                                                    start=True, stop=True))
    v_mm = P.op("tensor", lambda e, o=bb2[0:1, :]: e.matmul(o, sb.ones_col[:], sb.xsq[pr][:],
                                                            start=True, stop=True), inc="PE")
    state[("rp_xbf", pr)] = v_mm
    P.wait("vector", "PE", v_mm)
    rwg = state.get(("rp_rows", pr))
    if rwg is not None:
        P.wait("vector", "PE", rwg)
    P.op("vector", lambda e, i=bs[0:1, :]: e.tensor_scalar_mul(r(0), i, 1.0 / 128.0))
    P.op("vector", lambda e, i=bb2[0:1, :]: e.tensor_scalar(r(1), i, 1.0 / 128.0, LNEPS,
                                                            ALU.mult, ALU.add))
    P.op("vector", lambda e: e.tensor_mul(r(2), r(0), r(0)))
    P.op("vector", lambda e: e.tensor_sub(r(1), r(1), r(2)))
    v_rec = P.op("vector", lambda e: e.reciprocal(r(2), r(1)))
    P.wait("scalar", "DVE", v_rec)
    v_sq = P.op("scalar", lambda e: e.activation(r(1), r(2), AF.Sqrt))
    P.wait("vector", "ACT", v_sq)
    v_mr = P.op("vector", lambda e: e.tensor_mul(r(2), r(0), r(1)))
    P.wait("tensor", "DVE", v_mr)
    P.op("tensor", lambda e, o=bs[:, 0:TT]: e.matmul(o, sb.ones_row[0:1, :], r(1),
                                                     start=True, stop=True))
    v_bc = P.op("tensor", lambda e, o=bb2[:, 0:TT]: e.matmul(o, sb.ones_row[0:1, :], r(2),
                                                             start=True, stop=True), inc="PE")
    state[("rp_rows", pr)] = v_bc
    P.wait("vector", "PE", v_bc)
    P.op("vector", lambda e, b1=bs[:, 0:TT]: e.tensor_mul(q32, r32, b1))
    v_ap = P.op("vector", lambda e, b2=bb2[:, 0:TT]: e.tensor_sub(q32, q32, b2))
    state[("rp_bank", pr)] = v_ap
    v_gb = P.op("vector", lambda e, o=st: e.tensor_scalar(o, q32, sb.lncol[:, 0:1],
                                                          sb.lncol[:, 1:2],
                                                          ALU.mult, ALU.add))
    state[("rp_q32", pr)] = (v_gb, v_sw)
    return v_gb, v_c


def stage_kv_gather(P, att, pairs):
    for i in range(4):
        P.wait("gpsimd", f"SO{i}", P.cnt.get(f"SO{i}", 0))
    P.op("gpsimd", lambda e: e.collective_compute(
        "AllGather", ALU.bypass, replica_groups=pairs,
        ins=[att["CCKi"].ap()], outs=[att["CCKo"].ap()]), inc="CC", n=1, dma=True)
    P.op("gpsimd", lambda e: e.collective_compute(
        "AllGather", ALU.bypass, replica_groups=pairs,
        ins=[att["CCVi"].ap()], outs=[att["CCVo"].ap()]), inc="CC", n=1, dma=True)
    return P.cnt["CC"]


def stage_attn_core(P, sb, att, acfg, state, TC, S_, parity_sym, v_cc):
    _, _, _, qc_info = attn_geometry(S_)
    nqc = TC // TT
    nk = S_ // 128
    r = lambda i: sb.rows[0:1, i * TT:(i + 1) * TT]
    P.wait("sync", "CC", v_cc)
    for h in range(NHEADS):
        par = h % 2
        last = state.get(("kv_pe", par))
        if last is not None:
            P.wait("sync", "PE", last)
        qa = sb.qbuf[:, par * 2048: par * 2048 + TC]
        ka = sb.xn[1][:, par * 4096: par * 4096 + S_]
        va = sb.xn[0][:, par * 4096: par * 4096 + S_]
        P.dma("sync", qa, att["Qd"].ap()[h, :, :], f"AQ{par}")
        for rk in range(2):
            P.dma("sync", ka[:, rk * TC:(rk + 1) * TC],
                  att["CCKo"].ap()[rk * D + h * 128:(rk * D) + (h + 1) * 128, :], f"AQ{par}")
            src = att["CCVo"].ap()[rk * NHEADS * TC + h * TC: rk * NHEADS * TC + (h + 1) * TC, :]
            v_ld = P.dma("sync", va[:, rk * TC:(rk + 1) * TC].rearrange("p (j d) -> p j d", d=128),
                         src.rearrange("(j p) d -> p j d", p=128), f"AQ{par}")
        if acfg["variant"] == 0:
            P.wait("vector", f"AQ{par}", v_ld)
            P.op("vector", lambda e: e.memset(r(5)[0:1, 0:2], 0.0))
            for src_t, n_ch, slot in ((qa, TC // TT, 0), (ka, S_ // TT, 1)):
                for j in range(n_ch):
                    stt_ = sb.stage[j % 2][:]
                    lastq = state.get(("nrm_pe", j % 2))
                    if lastq is not None:
                        P.wait("scalar", "PE", lastq)
                    P.wait("scalar", f"AQ{par}", v_ld)
                    v_sq = P.op("scalar", lambda e, o=stt_, i=src_t[:, j * TT:(j + 1) * TT]:
                                e.activation(o, i, AF.Square))
                    P.wait("tensor", "ACT", v_sq)
                    P.wait("tensor", "DVE", P.cnt["DVE"])
                    v_n = P.op("tensor", lambda e, o=sb.ps[6][0:1, :], l=sb.ones_col[:], r2=stt_:
                               e.matmul(o, l, r2, start=True, stop=True), inc="PE")
                    state[("nrm_pe", j % 2)] = v_n
                    P.wait("vector", "PE", v_n)
                    P.op("vector", lambda e: e.reduce_max(r(4)[0:1, 0:1], sb.ps[6][0:1, :], axis=AX.X))
                    P.op("vector", lambda e, sl=slot: e.tensor_max(r(5)[0:1, sl:sl + 1],
                                                                   r(5)[0:1, sl:sl + 1],
                                                                   r(4)[0:1, 0:1]))
            P.op("vector", lambda e: e.tensor_add(r(4)[0:1, 0:1], r(5)[0:1, 0:1], r(5)[0:1, 1:2]))
            v_mh = P.op("vector", lambda e: e.tensor_scalar_mul(r(4)[0:1, 0:1], r(4)[0:1, 0:1], -0.5))
            P.wait("tensor", "DVE", v_mh)
            v_bb = P.op("tensor", lambda e: e.matmul(sb.ps[5][:, 0:1], sb.ones_row[0:1, :],
                                                     r(4)[0:1, 0:1], start=True, stop=True), inc="PE")
            P.wait("vector", "PE", v_bb)
            v_cb = P.op("vector", lambda e: e.tensor_copy(sb.colbuf[:, 0:1], sb.ps[5][:, 0:1]))
            state["bias_dve"] = v_cb
        P.wait("tensor", f"AQ{par}", v_ld)
        for qc in range(nqc):
            _, kbs = qc_info[(parity_sym, qc)]
            nkb_q = len(kbs)
            qp = qc % 2
            ob = sb.ps[2 + qp]
            db = sb.ps[4] if qp == 0 else sb.ps[6]
            bb = sb.ps[5] if qp == 0 else sb.ps[7]
            rr = r(3) if qp == 0 else r(2)

            def emit_st(i):
                g_i = kbs[i][0]
                stb = sb.ps[i % 2]
                lastb = state.get(("st_act", i % 2))
                if lastb is not None:
                    P.wait("tensor", lastb[0], lastb[1])
                return P.op("tensor",
                            lambda e, o=stb[:], l=ka[:, g_i * 128:(g_i + 1) * 128],
                            r2=qa[:, qc * TT:(qc + 1) * TT]:
                            e.matmul(o, l, r2, start=True, stop=True), inc="PE")

            v_st = {0: emit_st(0)}
            for idx, (g, mask_idx) in enumerate(kbs):
                stb = sb.ps[idx % 2]
                pst = sb.stage[2 + idx % 2][:]
                lastp = state.get(("pst_pe", idx % 2))
                P.wait("scalar", "PE", v_st[idx])
                if lastp is not None:
                    P.wait("scalar", "PE", lastp)
                if acfg["variant"] == 0:
                    P.wait("scalar", "DVE", state["bias_dve"])
                    v_e = P.op("scalar", lambda e, o=pst, i=stb[:]:
                               e.activation(o, i, AF.Exp, bias=sb.colbuf[:, 0:1]))
                    v_ready = ("ACT", v_e)
                    state[("st_act", idx % 2)] = v_ready
                else:
                    if idx % 2 == 0:
                        P.wait("scalar", "DVE", state.get("o_tmp_dve", 0))
                    tq = sb.tmp32[idx % 2][:]
                    v_s2 = P.op("scalar", lambda e, o=tq, i=stb[:]: e.activation(o, i, AF.Square))
                    state[("st_act", idx % 2)] = ("ACT", v_s2)
                    P.wait("gpsimd", "ACT", v_s2)
                    v_e = P.op("gpsimd", lambda e, o=pst, a=tq: e.tensor_mul(o, a, a))
                    v_ready = ("POOL", v_e)
                if mask_idx is not None:
                    if v_ready[0] != "POOL":
                        P.wait("gpsimd", v_ready[0], v_ready[1])
                    v_m = P.op("gpsimd", lambda e, o=pst, m=sb.msk[:, mask_idx * TT:(mask_idx + 1) * TT]:
                               e.tensor_mul(o, o, m))
                    v_ready = ("POOL", v_m)
                if idx + 1 < nkb_q:
                    v_st[idx + 1] = emit_st(idx + 1)
                P.wait("tensor", v_ready[0], v_ready[1])
                if idx == 0:
                    P.wait("tensor", "DVE", state.get(("den_read", qp), 0))
                    P.wait("tensor", "DVE", state.get(("o_evac", qp), 0))
                P.op("tensor", lambda e, o=db[0:1, :], l=sb.ones_col[:], r2=pst,
                     s=(idx == 0), t=(idx == nkb_q - 1): e.matmul(o, l, r2, start=s, stop=t))
                v_av = P.op("tensor",
                            lambda e, o=ob[:], l=va[:, g * 128:(g + 1) * 128], r2=pst,
                            s=(idx == 0), t=(idx == nkb_q - 1):
                            e.matmul(o, l, r2, start=s, stop=t), inc="PE")
                state[("pst_pe", idx % 2)] = v_av
            P.wait("vector", "PE", v_av)
            v_rc = P.op("vector", lambda e, o=rr, i=db[0:1, :]: e.reciprocal(o, i))
            state[("den_read", qp)] = v_rc
            P.wait("tensor", "DVE", v_rc)
            P.wait("tensor", "DVE", state.get(("bc_read", qp), 0))
            v_b2 = P.op("tensor", lambda e, o=bb[:, 0:TT], i=rr: e.matmul(o, sb.ones_row[0:1, :],
                                                                          i, start=True, stop=True),
                        inc="PE")
            o_sl = sb.hid[:, h * TC + qc * TT: h * TC + (qc + 1) * TT]
            P.wait("vector", "PE", v_b2)
            v_cp = P.op("vector", lambda e, i=ob[:]: e.tensor_copy(sb.tmp32[0][:], i))
            v_o = P.op("vector", lambda e, o=o_sl, b2=bb[:, 0:TT]: e.tensor_mul(o, sb.tmp32[0][:], b2))
            state["o_tmp_dve"] = v_o
            state[("o_evac", qp)] = v_cp
            state[("bc_read", qp)] = v_o
            state["o_done"] = v_o
        state[("kv_pe", par)] = P.cnt["PE"]
    return state["o_done"]


def stage_outproj_tile(P, sb, att, X_in, X_out, tt, acfg, state, TC):
    tsl = slice(tt * TT, (tt + 1) * TT)
    gate = state.get("op_gate")
    if gate is not None:
        P.wait("sync", gate[0], gate[1])
    xin_ap = X_in.ap()[:, :, tsl].rearrange("c p t -> p c t")
    v_x = P.dma("sync", sb.xt[:], xin_ap, "XL")
    v_o = state["o_done"]
    for dg in range(8):
        bpar = dg % 4
        w, wp = bpar // 2, bpar % 2
        buf = sb.wup[w][wp]
        last = state["wbuf_last_pe"].get(("o", bpar))
        if last is not None:
            P.wait("sync", "PE", last)
        src = att["Wo16"].ap().rearrange("(k p) ee -> p k ee", p=128)[:, :, dg * 256:(dg + 1) * 256]
        v_w = None
        for kg in range(4):
            v_w = P.dma("sync", buf[:].rearrange("p (k ee) -> p k ee", ee=256)[:, kg * 4:(kg + 1) * 4, :],
                        src[:, kg * 4:(kg + 1) * 4, :], f"WO{bpar}")
        for sub in range(2):
            dm = dg * 2 + sub
            bank = sb.ps[dm % 2]
            lastb = state["pso_last"].get(dm % 2)
            if lastb is not None:
                P.wait("tensor", "DVE", lastb)
            if sub == 0:
                P.wait("tensor", f"WO{bpar}", v_w)
                P.wait("tensor", "DVE", v_o)
            for hc in range(NHEADS):
                lhsT = buf[:, hc * 256 + sub * 128: hc * 256 + sub * 128 + 128]
                rhs = sb.hid[:, hc * TC + tt * TT: hc * TC + (tt + 1) * TT]
                vv = P.op("tensor",
                          lambda e, o=bank[:], l=lhsT, r2=rhs, s=(hc == 0), t=(hc == NHEADS - 1):
                          e.matmul(o, l, r2, start=s, stop=t),
                          inc="PE" if hc == NHEADS - 1 else None)
            if sub == 1:
                state["wbuf_last_pe"][("o", bpar)] = vv
            xc = sb.xt[:, dm * TT:(dm + 1) * TT]
            P.wait("vector", "PE", vv)
            P.wait("vector", "XL", v_x)
            v_ev = P.op("vector",
                        lambda e, o=xc, i=bank[:]:
                        e.scalar_tensor_tensor(o, i, 1.0, o, ALU.mult, ALU.add))
            state["pso_last"][dm % 2] = v_ev
    v_evac = P.cnt["DVE"]
    P.wait("gpsimd", "DVE", v_evac)
    xout_ap = X_out.ap()[:, :, tsl].rearrange("c p t -> p c t")
    v_store = P.dma("gpsimd", xout_ap, sb.xt[:], "XS")
    state["op_gate"] = ("XS", v_store)
    return v_store


def build_attn_launch(variant, pairs=None, S_=S):
    TC = S_ // 2
    ntt = TC // TT
    nc = bass.Bass("TRN2", target_bir_lowering=False)
    X_in = nc.dram_tensor("x_in", [DC, 128, TC], F32, kind="ExternalInput")
    X_out = nc.dram_tensor("x_out", [DC, 128, TC], F32, kind="ExternalOutput")
    WqkvT = nc.dram_tensor("wqkvt", [D, 3 * D], F32, kind="ExternalInput")
    WoT = nc.dram_tensor("wot", [D, D], F32, kind="ExternalInput")
    MSK = nc.dram_tensor("masks", [128, 4 * TT], BF16, kind="ExternalInput")
    EYE = nc.dram_tensor("eyec", [128, 128], BF16, kind="ExternalInput")
    SWP = nc.dram_tensor("swapc", [128, 128], F32, kind="ExternalInput")
    att = {}
    sb = Pool(nc)
    pro_list = [(sb.msk[:], MSK), (sb.eye[:], EYE)]
    if variant == 1:
        LNC = nc.dram_tensor("lncolc", [128, 2], F32, kind="ExternalInput")
        att["ropec"] = nc.dram_tensor("ropec", [128, TC], F32, kind="ExternalInput")
        att["ropes"] = nc.dram_tensor("ropes", [128, TC], F32, kind="ExternalInput")
        pro_list.append((sb.lncol[:], LNC))
        pro_list.append((sb.swp[:], SWP))
    att["Wq16"] = nc.dram_tensor("wq16", [D, 3 * D], BF16)
    att["Wo16"] = nc.dram_tensor("wo16", [D, D], BF16)
    att["Qd"] = nc.dram_tensor("qd", [NHEADS, 128, TC], BF16)
    att["CCKi"] = nc.dram_tensor("ccki", [D, TC], BF16)
    att["CCKo"] = nc.dram_tensor("ccko", [2 * D, TC], BF16)
    att["CCVi"] = nc.dram_tensor("ccvi", [NHEADS * TC, 128], BF16)
    att["CCVo"] = nc.dram_tensor("ccvo", [2 * NHEADS * TC, 128], BF16)

    acfg = {"variant": variant, "qk_scale": float(128.0 ** -0.25)}
    P = Prog(nc)
    pro = stage_prologue(P, sb, pro_list)
    state = fresh_state()
    stage_cast_weight(P, sb, WqkvT, att["Wq16"], D, 3 * D, state)
    stage_cast_weight(P, sb, WoT, att["Wo16"], D, D, state)
    for s in ("CS0", "CS1"):
        P.wait("sync", s, P.cnt.get(s, 0))
    P.wait("tensor", "LW", pro["consts"])
    P.wait("tensor", "DVE", pro["ones"])
    P.wait("scalar", "LW", pro["consts"])
    emit_attn_sublayer(P, sb, att, acfg, X_in, X_out, TC, S_, pairs, state)
    P.wait("sync", "XS", P.cnt["XS"])
    P.finish()
    return nc


def fresh_state():
    return {"wbuf_last_pe": {}, "wbuf_ready": {}, "psq_last": {}, "psgu_last": {},
            "stage_sg": {}, "tr_last": {}, "pso_last": {}, "wdn_last_pe": {}}


def soft_barrier(P):
    tot = {k: v for k, v in P.cnt.items() if v > 0}
    for eng in ENGS:
        for k, v in tot.items():
            P.wait(eng, k, v)


def emit_attn_sublayer(P, sb, att, acfg, X_in, X_out, TC, S_, pairs, state):
    import copy
    ntt = TC // TT
    gate = state.get("x_gate")
    for tt in range(ntt):
        par = tt % 2
        ln = stage_ln_tile(P, sb, X_in, tt, par, gate_prev=gate)
        stage_qkv_tile(P, sb, att, tt, par, ln["xn_done"], acfg, state, TC)
        gate = ("DVE", P.cnt["DVE"])
    if pairs is None:
        pairs = [[2 * i, 2 * i + 1] for i in range(NCORES // 2)]
    v_cc = stage_kv_gather(P, att, pairs)
    P.begin_switch()
    snap = copy.deepcopy(state)
    out_states = []
    for par_body in (0, 1):
        P.begin_body()
        st2 = copy.deepcopy(snap)
        stage_attn_core(P, sb, att, acfg, st2, TC, S_, par_body, v_cc)
        out_states.append(st2)
        P.end_body()
    P.end_switch()
    state.update({k: v for k, v in out_states[0].items()})
    state["op_gate"] = ("DVE", P.cnt["DVE"])
    v_store = None
    for tt in range(ntt):
        v_store = stage_outproj_tile(P, sb, att, X_in, X_out, tt, acfg, state, TC)
    state["x_gate"] = ("XS", v_store)


def emit_ffn_sublayer(P, sb, X_in, X_out, wcfg, state, ntt=NTT):
    gate = state.get("x_gate")
    v_store = None
    for tt in range(ntt):
        par = tt % 2
        ln = stage_ln_tile(P, sb, X_in, tt, par, gate_prev=gate)
        v_store = stage_ffn_tile(P, sb, X_out, tt, par, ln["xn_done"], wcfg, state)
        gate = ("XS", v_store)
    state["x_gate"] = gate


def build_fused_launch(S_=S, pairs=None):
    TC = S_ // 2
    nc = bass.Bass("TRN2", target_bir_lowering=False)
    X_in = nc.dram_tensor("x_in", [DC, 128, TC], F32, kind="ExternalInput")
    X_out = nc.dram_tensor("x_out", [DC, 128, TC], F32, kind="ExternalOutput")
    XA = nc.dram_tensor("xa", [DC, 128, TC], F32)
    XB = nc.dram_tensor("xb", [DC, 128, TC], F32)
    W1T = nc.dram_tensor("w1t", [D, H0], F32, kind="ExternalInput")
    W3T = nc.dram_tensor("w3t", [D, H0], F32, kind="ExternalInput")
    W2T = nc.dram_tensor("w2t", [H0, D], F32, kind="ExternalInput")
    WQ0 = nc.dram_tensor("wqkvt0", [D, 3 * D], F32, kind="ExternalInput")
    WO0 = nc.dram_tensor("wot0", [D, D], F32, kind="ExternalInput")
    GT = nc.dram_tensor("gt", [D, H1], F32, kind="ExternalInput")
    UT = nc.dram_tensor("ut", [D, H1], F32, kind="ExternalInput")
    DT = nc.dram_tensor("dt", [H1, D], F32, kind="ExternalInput")
    WQ1 = nc.dram_tensor("wqkvt1", [D, 3 * D], F32, kind="ExternalInput")
    WO1 = nc.dram_tensor("wot1", [D, D], F32, kind="ExternalInput")
    SCL = nc.dram_tensor("scl", [128, 3 * 64], F32, kind="ExternalInput")
    LNC = nc.dram_tensor("lncolc", [128, 2], F32, kind="ExternalInput")
    ROPEC = nc.dram_tensor("ropec", [128, TC], F32, kind="ExternalInput")
    ROPES = nc.dram_tensor("ropes", [128, TC], F32, kind="ExternalInput")
    MSK = nc.dram_tensor("masks", [128, 4 * TT], BF16, kind="ExternalInput")
    EYE = nc.dram_tensor("eyec", [128, 128], BF16, kind="ExternalInput")
    SWP = nc.dram_tensor("swapc", [128, 128], F32, kind="ExternalInput")
    H128c = nc.dram_tensor("h128c", [128, 128], BF16, kind="ExternalInput")
    H64c = nc.dram_tensor("h64c", [128, 128], BF16, kind="ExternalInput")
    W1T16 = nc.dram_tensor("w1t16", [D, H0], BF16)
    W3T16 = nc.dram_tensor("w3t16", [D, H0], BF16)
    W2A = nc.dram_tensor("w2a", [H0, D], BF16)
    W2pp = nc.dram_tensor("w2pp", [H0, D], BF16)
    GT16 = nc.dram_tensor("gt16", [D, H1], BF16)
    UT16 = nc.dram_tensor("ut16", [D, H1], BF16)
    DT16 = nc.dram_tensor("dt16", [H1, D], BF16)
    WQ016 = nc.dram_tensor("wq016", [D, 3 * D], BF16)
    WO016 = nc.dram_tensor("wo016", [D, D], BF16)
    WQ116 = nc.dram_tensor("wq116", [D, 3 * D], BF16)
    WO116 = nc.dram_tensor("wo116", [D, D], BF16)
    att_common = {
        "Qd": nc.dram_tensor("qd", [NHEADS, 128, TC], BF16),
        "CCKi": nc.dram_tensor("ccki", [D, TC], BF16),
        "CCKo": nc.dram_tensor("ccko", [2 * D, TC], BF16),
        "CCVi": nc.dram_tensor("ccvi", [NHEADS * TC, 128], BF16),
        "CCVo": nc.dram_tensor("ccvo", [2 * NHEADS * TC, 128], BF16),
    }
    att0 = dict(att_common, Wq16=WQ016, Wo16=WO016)
    att1 = dict(att_common, Wq16=WQ116, Wo16=WO116, ropec=ROPEC, ropes=ROPES)

    sb = Pool(nc)
    P = Prog(nc)
    pro = stage_prologue(P, sb, [(sb.eye[:], EYE), (sb.swp[:], SWP),
                                 (sb.lncol[:], LNC), (sb.scale_cols[:], SCL),
                                 (sb.msk[:, 0:128], H128c), (sb.msk[:, 128:256], H64c)])
    state = fresh_state()
    P.wait("tensor", "LW", pro["consts"])
    P.wait("tensor", "DVE", pro["ones"])
    P.wait("scalar", "LW", pro["consts"])

    def direct_cast(src, dst, rows, sem):
        v = None
        for r0 in range(0, rows, rows // 2):
            v = P.dma("gpsimd", dst.ap()[r0:r0 + rows // 2, :],
                      src.ap()[r0:r0 + rows // 2, :], sem)
        return v

    v_w13 = direct_cast(W1T, W1T16, D, "CW")
    v_w13 = direct_cast(W3T, W3T16, D, "CW")
    stage_w2_transform(P, sb, W2T, W2A, W2pp, sb.msk[:, 0:128], sb.msk[:, 128:256], state)
    for s_f32, s_bf16, rr_ in ((WQ0, WQ016, D), (WO0, WO016, D), (GT, GT16, D),
                               (UT, UT16, D), (DT, DT16, H1), (WQ1, WQ116, D),
                               (WO1, WO116, D)):
        direct_cast(s_f32, s_bf16, rr_, "CW2")
    P.wait("sync", "CW", v_w13)
    soft_barrier_exclude = ("CW2",)
    for s in ("TS0", "TS1"):
        P.wait("sync", s, P.cnt.get(s, 0))
    v_msk = P.dma("sync", sb.msk[:], MSK.ap(), "LW")
    P.wait("tensor", "LW", v_msk)
    P.wait("scalar", "LW", v_msk)
    P.wait("vector", "LW", v_msk)
    deferred_casts = {}

    acfg0 = {"variant": 0, "qk_scale": float(128.0 ** -0.25)}
    acfg1 = {"variant": 1, "qk_scale": float(128.0 ** -0.25)}
    wcfg0 = dict(Wg=W1T16, Wu=W3T16, Wdn=W2pp, HC=H0C,
                 g_scale=None, u_scale=None, d_scale=None)
    wcfg1 = dict(Wg=GT16, Wu=UT16, Wdn=DT16, HC=H1C,
                 g_scale=lambda c: sb.scale_cols[:, c:c + 1],
                 u_scale=lambda c: sb.scale_cols[:, 64 + c:64 + c + 1],
                 d_scale=lambda c: sb.scale_cols[:, 128 + c:128 + c + 1])

    chain = [X_in, XA, XB, XA, XB, XA, X_out]
    sublayers = [
        ("ffn", wcfg0, acfg0), ("attn", att0, acfg0), ("ffn", wcfg0, acfg0),
        ("ffn", wcfg1, acfg1), ("attn", att1, acfg1), ("ffn", wcfg1, acfg1),
    ]
    for i, (kind, cfg, acfg) in enumerate(sublayers):
        if i > 0:
            soft_barrier(P)
        st = fresh_state()
        if kind == "ffn":
            emit_ffn_sublayer(P, sb, chain[i], chain[i + 1], cfg, st, ntt=TC // TT)
        else:
            emit_attn_sublayer(P, sb, cfg, acfg, chain[i], chain[i + 1], TC, S_, pairs, st)
        for args in deferred_casts.pop(i, []):
            s_f32, s_bf16, rr_, cc_, stg = args
            stage_cast_weight(P, sb, s_f32, s_bf16, rr_, cc_, st, staging=stg)
    P.wait("sync", "XS", P.cnt["XS"])
    P.finish()
    return nc


def core_positions(core, S_=S):
    BLK_ = S_ // 4
    blocks = ZIG[core % 2]
    return np.concatenate([np.arange(bb * BLK_, (bb + 1) * BLK_) for bb in blocks])


def host_rope(pos):
    d = np.arange(0, HD, 2, dtype=np.float32) / np.float32(HD)
    inv = (np.float32(1.0) / (np.float32(10000.0) ** d)).astype(np.float32)
    t = np.outer(inv, pos.astype(np.float32)).astype(np.float32)
    return np.cos(t).astype(np.float32), np.sin(t).astype(np.float32)


def host_rope_tables(pos):
    c, s = host_rope(pos)
    C2 = np.vstack([c, c]).astype(np.float32)
    S2 = np.vstack([-s, s]).astype(np.float32)
    return C2, S2


def host_swap():
    m = np.zeros((128, 128), np.float32)
    for p in range(128):
        m[(p + 64) % 128, p] = 1.0
    return m


def host_masks():
    m = np.zeros((128, 4 * TT), np.float32)
    for i in range(4):
        for p in range(128):
            j0 = p + 128 * i
            if j0 < TT:
                m[p, i * TT + j0:(i + 1) * TT] = 1.0
    return m.astype(ml_dtypes.bfloat16)


def hadamard(n):
    h = np.array([[1.0]], dtype=np.float32)
    while h.shape[0] < n:
        h = np.block([[h, h], [h, -h]])
    return h


def host_constants():
    h128 = hadamard(128).astype(ml_dtypes.bfloat16)
    h64 = hadamard(64)
    h64blk = np.zeros((128, 128), np.float32)
    h64blk[:64, :64] = h64
    h64blk[64:, 64:] = h64
    return {"h128c": h128, "h64c": h64blk.astype(ml_dtypes.bfloat16)}


def shard_x(x):
    out = []
    for core in range(NCORES):
        b = core // 2
        blocks = ZIG[core % 2]
        toks = np.concatenate([x[b, bb * BLK:(bb + 1) * BLK, :] for bb in blocks], axis=0)
        fm = np.ascontiguousarray(toks.T.reshape(DC, 128, TCORE))
        out.append(fm)
    return out


def unshard_x(shards):
    x = np.zeros((B, S, D), np.float32)
    for core in range(NCORES):
        b = core // 2
        blocks = ZIG[core % 2]
        toks = shards[core].reshape(D, TCORE).T
        for i, bb in enumerate(blocks):
            x[b, bb * BLK:(bb + 1) * BLK, :] = toks[i * BLK:(i + 1) * BLK, :]
    return x


def ffn1_scale_cols(gate_s, up_s, down_s):
    scl = np.zeros((128, 3 * 64), np.float32)
    scl[:, 0:H1C] = gate_s.reshape(H1C, 128).T
    scl[:, 64:64 + H1C] = up_s.reshape(H1C, 128).T
    scl[:, 128:128 + DC] = (down_s * 0.5).reshape(DC, 128).T
    return scl


_CACHE = {}


def _built(key, fn):
    if key not in _CACHE:
        _CACHE[key] = fn()
    return _CACHE[key]


def kernel(**inputs):
    f32 = lambda a: np.ascontiguousarray(np.asarray(a), dtype=np.float32)
    x = f32(inputs["L0_attn_x"])
    shards = shard_x(x)
    consts = host_constants()
    masks = host_masks()
    eye = np.eye(128, dtype=np.float32).astype(ml_dtypes.bfloat16)
    cores = list(range(NCORES))

    w1t = f32(inputs["L0_ffn_W1"]).T.copy()
    w3t = f32(inputs["L0_ffn_W3"]).T.copy()
    w2t = f32(inputs["L0_ffn_W2"]).T.copy()
    wqkvt0 = f32(inputs["L0_attn_W_qkv"]).T.copy()
    wot0 = f32(inputs["L0_attn_W_o"]).T.copy()
    gt = f32(inputs["L1_ffn_gate_w"]).T.copy()
    ut = f32(inputs["L1_ffn_up_w"]).T.copy()
    dt = f32(inputs["L1_ffn_down_w"]).T.copy()
    scl = ffn1_scale_cols(f32(inputs["L1_ffn_gate_s"]),
                          f32(inputs["L1_ffn_up_s"]),
                          f32(inputs["L1_ffn_down_s"]))
    wqkvt1 = f32(inputs["L1_attn_w_qkv"]).copy()
    wot1 = f32(inputs["L1_attn_w_out"]).copy()
    lncol = np.stack([f32(inputs["L1_attn_ln_scale"]),
                      f32(inputs["L1_attn_ln_bias"])], axis=1).copy()
    ropecs = [host_rope_tables(core_positions(c)) for c in cores]
    swapc = host_swap()

    def run(nc, per_core):
        res = run_bass_kernel_spmd(nc, per_core, core_ids=cores)
        return [np.asarray(res.results[c]["x_out"], dtype=np.float32) for c in cores]

    def ffn0(shards):
        nc = _built("ffn0", build_ffn0_launch)
        return run(nc, [{"x_in": shards[c], "w1t": w1t, "w3t": w3t, "w2t": w2t,
                         "h128c": consts["h128c"], "h64c": consts["h64c"]}
                        for c in cores])

    def ffn1(shards):
        nc = _built("ffn1", build_ffn1_launch)
        return run(nc, [{"x_in": shards[c], "gt": gt, "ut": ut, "dt": dt, "scl": scl}
                        for c in cores])

    def attn0(shards):
        nc = _built("attn0", lambda: build_attn_launch(0))
        return run(nc, [{"x_in": shards[c], "wqkvt": wqkvt0, "wot": wot0,
                         "masks": masks, "eyec": eye, "swapc": swapc} for c in cores])

    def attn1(shards):
        nc = _built("attn1", lambda: build_attn_launch(1))
        return run(nc, [{"x_in": shards[c], "wqkvt": wqkvt1, "wot": wot1,
                         "masks": masks, "eyec": eye, "swapc": swapc, "lncolc": lncol,
                         "ropec": ropecs[c][0], "ropes": ropecs[c][1]}
                        for c in cores])

    import os
    if os.environ.get("MK_UNFUSED"):
        shards = ffn0(shards)
        shards = attn0(shards)
        shards = ffn0(shards)
        shards = ffn1(shards)
        shards = attn1(shards)
        shards = ffn1(shards)
        return unshard_x(shards)

    nc = _built("fused", build_fused_launch)
    per_core = []
    for c in cores:
        per_core.append({
            "x_in": shards[c],
            "w1t": w1t, "w3t": w3t, "w2t": w2t,
            "wqkvt0": wqkvt0, "wot0": wot0,
            "gt": gt, "ut": ut, "dt": dt, "scl": scl,
            "wqkvt1": wqkvt1, "wot1": wot1, "lncolc": lncol,
            "ropec": ropecs[c][0], "ropes": ropecs[c][1],
            "masks": masks, "eyec": eye, "swapc": swapc,
            "h128c": consts["h128c"], "h64c": consts["h64c"],
        })
    res = run_bass_kernel_spmd(nc, per_core, core_ids=cores)
    return unshard_x([np.asarray(res.results[c]["x_out"], dtype=np.float32) for c in cores])
